# version 11
# baseline (speedup 1.0000x reference)
import math
from contextlib import ExitStack
from dataclasses import dataclass

import numpy as np
import ml_dtypes

import concourse.bass as bass
import concourse.mybir as mybir
from concourse.bass_utils import run_bass_kernel_spmd

F32 = mybir.dt.float32
BF16 = mybir.dt.bfloat16
I32 = mybir.dt.int32
AF = mybir.ActivationFunctionType
OP = mybir.AluOpType
AX = mybir.AxisListType
BF16NP = ml_dtypes.bfloat16
NEG = -1.0e30
DBG = {}


@dataclass
class Cfg:
    b: int = 4
    s: int = 4096
    d: int = 2048
    dff: int = 16384
    ha: int = 32
    kva: int = 8
    dha: int = 64
    win: int = 1024
    hb: int = 16
    dhb: int = 128
    sublayers: tuple = ("attn_a", "ffn", "attn_b", "ffn")
    hg: int = 4

    @property
    def half(self): return self.s // 2
    @property
    def tpc(self): return self.b * self.half // 8
    @property
    def dc(self): return self.d // 128
    @property
    def nqg(self): return self.tpc // 512
    @property
    def nblk(self): return self.half // 512
    @property
    def dka(self): return self.kva * self.dha
    @property
    def dqa(self): return self.ha * self.dha
    @property
    def dvb(self): return self.hb * self.dhb
    @property
    def nkcg(self): return self.s // 128


def sigma_perm(n):
    p = []
    for c in range(n // 256):
        p.extend(range(256 * c, 256 * c + 256, 2))
        p.extend(range(256 * c + 1, 256 * c + 256, 2))
    return np.array(p, dtype=np.int64)


def blocks_of(cfg, parity, p):
    nb = cfg.nblk
    k = cfg.tpc // 512
    if nb == 1:
        return [0]
    cand = [0, nb - 1, 2, nb - 3] if parity == 0 else [1, nb - 2, 3, nb - 4]
    return sorted(cand[:k])


def q0_of(cfg, parity, p, qg):
    return p * cfg.half + blocks_of(cfg, parity, p)[qg] * 512


def owner_of(cfg, pos):
    p = pos // cfg.half
    j = (pos % cfg.half) // 512
    for rh in (0, 1):
        bl = blocks_of(cfg, rh, p)
        if j in bl:
            return p, rh, bl.index(j) * 512 + (pos % 512)
    raise AssertionError(pos)


def kblocks(cfg, kind, p, qg):
    kblk = 256 if kind == "a" else 128
    nkb = 0
    for par in (0, 1):
        q0 = q0_of(cfg, par, p, qg)
        klo = (max(0, q0 - cfg.win) // kblk) * kblk if kind == "a" else 0
        nkb = max(nkb, (q0 + 512 - klo) // kblk)
    return kblk, nkb


def slopes_a(cfg):
    r = 2.0 ** (-8.0 / cfg.ha)
    return [r ** i for i in range(cfg.ha)]


def slopes_b(cfg):
    def _al(n):
        def pow2(m):
            start = 2.0 ** (-(2.0 ** (-(math.log2(m) - 3))))
            return [start * start ** i for i in range(m)]
        if math.log2(n).is_integer():
            return pow2(n)
        c = 2 ** math.floor(math.log2(n))
        return pow2(c) + _al(2 * c)[0::2][: n - c]
    return _al(cfg.hb)


def active_kinds(cfg):
    return [k[5] for k in cfg.sublayers if k.startswith("attn_")]


PAD = 4096


def iter_qgs(cfg):
    for pidx in range(2):
        for sl in cfg.sublayers:
            if sl.startswith("attn_"):
                kind = sl[5]
                for qg in range(cfg.nqg):
                    yield pidx, kind, qg


def build_tab(cfg, parity):
    tab = []
    for pidx, kind, qg in iter_qgs(cfg):
        dk = cfg.dka if kind == "a" else cfg.dvb
        q0 = q0_of(cfg, parity, pidx, qg)
        kblk, nkb = kblocks(cfg, kind, pidx, qg)
        W = nkb * kblk
        start = PAD + q0 + 512 - W
        tab.append(start)
        tab.append(start * dk)
    return np.asarray(tab, np.int32) if tab else np.zeros(2, np.int32)


def build_bias(cfg, parity):
    f = np.float32
    sa = np.array(slopes_a(cfg), f)
    sb = np.array(slopes_b(cfg), f)
    br = {"a": [], "b": []}
    sc = {"a": [], "b": []}
    for pidx, kind, qg in iter_qgs(cfg):
        H = cfg.ha if kind == "a" else cfg.hb
        sl = sa if kind == "a" else sb
        q0 = q0_of(cfg, parity, pidx, qg)
        kblk, nkb = kblocks(cfg, kind, pidx, qg)
        ncpb = kblk // 128
        for kbi in range(nkb):
            kb = q0 + 512 - (nkb - kbi) * kblk
            if kb < 0:
                br[kind].append(np.full((H, kblk), NEG, f))
                sc[kind].append(np.full((128, H * ncpb), NEG, f))
            else:
                pos = kb + np.arange(kblk, dtype=f)
                br[kind].append(sl[:, None] * pos[None, :])
                cpos = (kb + 128 * np.arange(ncpb, dtype=f)[None, :, None]
                        + np.arange(128, dtype=f)[:, None, None])
                sc[kind].append((sl[None, None, :] * cpos).transpose(0, 2, 1).reshape(128, H * ncpb))
    out = []
    for kind in ("a", "b"):
        out.append(np.concatenate([x.reshape(-1) for x in br[kind]]) if br[kind] else np.zeros(1, f))
        out.append(np.concatenate([x.reshape(-1) for x in sc[kind]]) if sc[kind] else np.zeros(1, f))
    return out


def tile4(wT, nj, ncc):
    K, M = wT.shape
    assert K == ncc * 128 and M == nj * 128, (wT.shape, nj, ncc)
    return np.ascontiguousarray(
        wT.reshape(ncc, 128, nj, 128).transpose(2, 1, 0, 3).astype(np.float32))


def vtile(wT, ndvs, ncc):
    return np.ascontiguousarray(
        wT.reshape(ncc, 128, ndvs, 512).transpose(2, 1, 0, 3).astype(np.float32))


def host_prep(cfg, inp):
    s2 = sigma_perm(cfg.d)
    s3 = sigma_perm(cfg.dff)
    f = np.float32
    prep = {"s2": s2}

    x = np.asarray(inp["L0_attn_a_x"], f)
    xp = np.ascontiguousarray(x[:, :, s2])
    xin = []
    for c in range(8):
        seq = c // 2
        passes = []
        for p in range(2):
            bl = blocks_of(cfg, c % 2, p)
            rows = [xp[seq, p * cfg.half + j * 512: p * cfg.half + (j + 1) * 512] for j in bl]
            passes.append(np.concatenate(rows, 0))
        xin.append(np.ascontiguousarray(np.stack(passes, 0)))
    prep["xin"] = xin

    dc = cfg.dc
    kinds = active_kinds(cfg)
    if "a" in kinds:
        prep["wqa"] = tile4(np.asarray(inp["L0_attn_a_wq"], f)[:, s2].T, cfg.dqa // 128, dc)
        prep["wka"] = tile4(np.asarray(inp["L0_attn_a_wk"], f)[:, s2].T, cfg.dka // 128, dc)
        prep["wva"] = vtile(np.asarray(inp["L0_attn_a_wv"], f)[:, s2].T, cfg.dka // 512, dc)
        prep["woa"] = tile4(np.ascontiguousarray(np.asarray(inp["L0_attn_a_wo"], f)[s2, :].T), dc, cfg.dqa // 128)
        prep["qnw"] = np.asarray(inp["L0_attn_a_q_norm_w"], f)
        prep["knw"] = np.asarray(inp["L0_attn_a_k_norm_w"], f)
    if "b" in kinds:
        prep["wqb"] = tile4(np.asarray(inp["L0_attn_b_Wq"], f)[:, s2].T, cfg.dvb // 128, dc)
        prep["wkb"] = tile4(np.asarray(inp["L0_attn_b_Wk"], f)[:, s2].T, cfg.dvb // 128, dc)
        prep["wvb"] = vtile(np.asarray(inp["L0_attn_b_Wv"], f)[:, s2].T, cfg.dvb // 512, dc)
        prep["wob"] = tile4(np.ascontiguousarray(np.asarray(inp["L0_attn_b_Wo"], f)[s2, :].T), dc, cfg.dvb // 128)
    if "ffn" in cfg.sublayers:
        def shard_gu(wp, sc):
            P_ = np.ascontiguousarray(np.asarray(wp).astype(np.int32).T[:, s3])
            S_ = np.ascontiguousarray(np.repeat(np.asarray(sc, f).T, 16, 0)[:, s3])
            return ([np.ascontiguousarray(P_[128 * c:128 * (c + 1)]) for c in range(8)],
                    [np.ascontiguousarray(S_[128 * c:128 * (c + 1)]) for c in range(8)])
        prep["pg"], prep["sg"] = shard_gu(inp["L0_ffn_gate_W_packed"], inp["L0_ffn_gate_scales"])
        prep["pu"], prep["su"] = shard_gu(inp["L0_ffn_up_W_packed"], inp["L0_ffn_up_scales"])
        Pd = np.ascontiguousarray(np.asarray(inp["L0_ffn_down_W_packed"]).astype(np.int32).T[:, s2])
        Sd = np.ascontiguousarray(np.repeat(np.asarray(inp["L0_ffn_down_scales"], f).T, 16, 0)[:, s2])
        nr = Pd.shape[0] // 8
        prep["pd"] = [np.ascontiguousarray(Pd[nr * c:nr * (c + 1)]) for c in range(8)]
        prep["sd"] = [np.ascontiguousarray(Sd[nr * c:nr * (c + 1)]) for c in range(8)]
    prep["idf"] = np.eye(128, dtype=f)
    prep["tab"] = [build_tab(cfg, c % 2) for c in range(8)]
    bb = [build_bias(cfg, c % 2) for c in range(8)]
    prep["bra"] = [b[0] for b in bb]
    prep["sca"] = [b[1] for b in bb]
    prep["brb"] = [b[2] for b in bb]
    prep["scb"] = [b[3] for b in bb]
    return prep


class Sem:
    def __init__(self, nc, ctx, name):
        self.h = ctx.enter_context(nc.semaphore(name))
        self.n = 0

    def inc(self, ins, k=1):
        ins.then_inc(self.h, k)
        self.n += k
        return self.n

    def dinc(self, ins):
        return self.inc(ins, 16)

    def wait(self, eng, v=None):
        eng.wait_ge(self.h, self.n if v is None else v)


def build_program(cfg):
    nc = bass.Bass("TRN2", target_bir_lowering=False, dynamic_dma_scratch_size=16384,
                   detect_race_conditions=False)
    dc, tpc, nqg, dff = cfg.dc, cfg.tpc, cfg.nqg, cfg.dff
    nth = tpc // 512
    ntt = tpc // 128
    kinds = active_kinds(cfg)
    has_ffn = "ffn" in cfg.sublayers

    xin = nc.dram_tensor("xin", [2, tpc, cfg.d], F32, kind="ExternalInput").ap()
    xout = nc.dram_tensor("xout", [2, tpc, cfg.d], F32, kind="ExternalOutput").ap()
    W = {}
    if "a" in kinds:
        W["wqa"] = nc.dram_tensor("wqa", [cfg.dqa // 128, 128, dc, 128], F32, kind="ExternalInput").ap()
        W["wka"] = nc.dram_tensor("wka", [cfg.dka // 128, 128, dc, 128], F32, kind="ExternalInput").ap()
        W["wva"] = nc.dram_tensor("wva", [cfg.dka // 512, 128, dc, 512], F32, kind="ExternalInput").ap()
        W["woa"] = nc.dram_tensor("woa", [dc, 128, cfg.dqa // 128, 128], F32, kind="ExternalInput").ap()
        qnw = nc.dram_tensor("qnw", [cfg.dqa], F32, kind="ExternalInput").ap()
        knw = nc.dram_tensor("knw", [cfg.dka], F32, kind="ExternalInput").ap()
    if "b" in kinds:
        W["wqb"] = nc.dram_tensor("wqb", [cfg.dvb // 128, 128, dc, 128], F32, kind="ExternalInput").ap()
        W["wkb"] = nc.dram_tensor("wkb", [cfg.dvb // 128, 128, dc, 128], F32, kind="ExternalInput").ap()
        W["wvb"] = nc.dram_tensor("wvb", [cfg.dvb // 512, 128, dc, 512], F32, kind="ExternalInput").ap()
        W["wob"] = nc.dram_tensor("wob", [dc, 128, cfg.dvb // 128, 128], F32, kind="ExternalInput").ap()
    if has_ffn:
        pg = nc.dram_tensor("pg", [128, dff], I32, kind="ExternalInput").ap()
        sg = nc.dram_tensor("sg", [128, dff], F32, kind="ExternalInput").ap()
        pu = nc.dram_tensor("pu", [128, dff], I32, kind="ExternalInput").ap()
        su = nc.dram_tensor("su", [128, dff], F32, kind="ExternalInput").ap()
        ndr = dff // 2 // 8
        pdt = nc.dram_tensor("pd", [ndr, cfg.d], I32, kind="ExternalInput").ap()
        sdt = nc.dram_tensor("sd", [ndr, cfg.d], F32, kind="ExternalInput").ap()
    idf = nc.dram_tensor("idf", [128, 128], F32, kind="ExternalInput").ap()
    ntab = max(4, len(build_tab(cfg, 0)))
    tabt = nc.dram_tensor("tab", [1, ntab], I32, kind="ExternalInput").ap()
    b0 = build_bias(cfg, 0)
    bias_in = {}
    if "a" in kinds:
        bias_in["bra"] = nc.dram_tensor("bra", [max(1, len(b0[0]))], F32, kind="ExternalInput").ap()
        bias_in["sca"] = nc.dram_tensor("sca", [max(1, len(b0[1]))], F32, kind="ExternalInput").ap()
    if "b" in kinds:
        bias_in["brb"] = nc.dram_tensor("brb", [max(1, len(b0[2]))], F32, kind="ExternalInput").ap()
        bias_in["scb"] = nc.dram_tensor("scb", [max(1, len(b0[3]))], F32, kind="ExternalInput").ap()

    if has_ffn:
        ngu_rows = (dff // 128) * 128
        nclr = dff // 128 // 8
        wg_in = nc.dram_tensor("wg_in", [ngu_rows, 256], BF16)
        wu_in = nc.dram_tensor("wu_in", [ngu_rows, 256], BF16)
        wd_in = nc.dram_tensor("wd_in", [dc * 128, nclr * 128], BF16)
        wg_full = nc.dram_tensor("wg_full", [8 * ngu_rows, 256], BF16)
        wu_full = nc.dram_tensor("wu_full", [8 * ngu_rows, 256], BF16)
        wd_full = nc.dram_tensor("wd_full", [8 * dc * 128, nclr * 128], BF16)
    kv = {}
    for kind in kinds:
        dk = cfg.dka if kind == "a" else cfg.dvb
        wmax = max(nkb * kblk for (kblk, nkb) in
                   [kblocks(cfg, kind, p, qg) for p in range(2) for qg in range(nqg)])
        kv[kind] = dict(
            wmax=wmax,
            k_in=[nc.dram_tensor(f"k{kind}_in{p}", [dk, tpc], BF16) for p in range(2)],
            v_in=[nc.dram_tensor(f"v{kind}_in{p}", [tpc, dk], BF16) for p in range(2)],
            k_out=nc.dram_tensor(f"k{kind}_out", [2, 2 * dk, tpc], BF16),
            v_out=nc.dram_tensor(f"v{kind}_out", [2, 2 * tpc, dk], BF16),
            kg=nc.dram_tensor(f"k{kind}_g", [dk, PAD + cfg.s], BF16),
            vg=nc.dram_tensor(f"v{kind}_g", [PAD + cfg.s, dk], BF16),
            kwin=nc.dram_tensor(f"k{kind}_w", [dk, wmax], BF16),
            vwin=nc.dram_tensor(f"v{kind}_w", [wmax, dk], BF16),
        )
    lnrow = nc.dram_tensor("lnrow", [96, 1024], F32)
    mtd = nc.dram_tensor("mtd", [48, 128, 128], F32)
    dbg_atb = nc.dram_tensor("dbg_atb", [16, 128, 512], BF16) if DBG.get("atb") else None
    dbg_t = (nc.dram_tensor("dbgt", [16, 128, 512], F32, kind="ExternalOutput")
             if (DBG.get("dump") or DBG.get("dump_ao") or DBG.get("dump_xt") or DBG.get("dump_xt2") or DBG.get("dump_wg")) else None)
    slot = {"ln": 0, "mt": 0}

    ctx = ExitStack()
    with ctx:
        def sbuf(name, shape, dt):
            return ctx.enter_context(nc.sbuf_tensor(name, shape, dt))

        XT = sbuf("XT", [128, dc * tpc], F32)
        XN = sbuf("XN", [128, dc * tpc], BF16)
        IB = sbuf("IB", [128, 16 * tpc], BF16)
        QT = IB
        AO = XN[:, 0:dc * 512]
        onesf = sbuf("onesf", [128, 1], F32)
        idt = sbuf("idt", [128, 128], F32)
        qnw_sb = sbuf("qnw_sb", [128, 16], F32)
        knw_sb = sbuf("knw_sb", [128, 16], F32)
        skcb = sbuf("skcb", [128, 256], F32)
        icol = sbuf("icol", [128, 8], I32)
        tab_sb = sbuf("tab_sb", [1, ntab], I32)
        aw = sbuf("aw", [128, max(4096, dc * 256)], F32)
        _w0 = aw[:, 0:1024].bitcast(BF16)
        _w1 = aw[:, 1024:2048].bitcast(BF16)
        wsl = [_w0, _w0]
        wsl2 = [_w1, _w1]
        wvsl = aw[:, 0:dc * 256].bitcast(BF16)
        kstw = max([cfg.kva * 256 if "a" in kinds else 0,
                    (cfg.dvb // 128) * 128 if "b" in kinds else 0])
        vstw = max([2 * cfg.dka if "a" in kinds else 0, 1 * cfg.dvb if "b" in kinds else 0])
        need_kv = 2 * (kstw + vstw) * 2
        asz = max(3584, (need_kv + 3) // 4)
        ast = sbuf("ast", [128, asz], F32)
        dq_i = ast[:, 0:512].bitcast(I32)
        dq_s = ast[:, 512:1024]
        dq_t = [ast[:, 1024 + 512 * i:1024 + 512 * i + 512].bitcast(I32) for i in range(2)]
        dq_b = ast[:, 2048:2304].bitcast(BF16)
        dq_sh = ast[:, 2304:2816].bitcast(I32)
        kst = [ast[:, (kstw // 2) * i:(kstw // 2) * (i + 1)].bitcast(BF16) for i in range(2)]
        vst = [ast[:, kstw + (vstw // 2) * i: kstw + (vstw // 2) * (i + 1)].bitcast(BF16) for i in range(2)]
        a_s = sbuf("a_s", [128, max(2 * cfg.d, 4096)], F32)
        stg = [a_s[:, cfg.d * i:cfg.d * (i + 1)] for i in range(2)]
        mxp = a_s[:, 0:2048]
        psp = a_s[:, 2048:4096]
        assert kstw // 2 + vstw // 2 <= 2048
        kst4 = kst + [a_s[:, 0:kstw // 2].bitcast(BF16),
                      a_s[:, 2048:2048 + kstw // 2].bitcast(BF16)]
        vst4 = vst + [a_s[:, kstw // 2:kstw // 2 + vstw // 2].bitcast(BF16),
                      a_s[:, 2048 + kstw // 2:2048 + kstw // 2 + vstw // 2].bitcast(BF16)]
        ssc = [sbuf(f"ssc{i}", [128, 512], F32) for i in range(4)]
        atb = [sbuf(f"atb{i}", [128, 512], BF16) for i in range(4)]
        sqs = [ssc[0], ssc[1]]
        kvstg = [atb[1], atb[2]]
        scrapb = atb[0]
        ab = sbuf("ab", [128, 2048], F32)
        bias2 = [ab[:, 512 * i:512 * i + 512] for i in range(4)]
        nmld = bias2
        nmb = ssc[3][:, 0:128]
        ldb = ssc[3][:, 128:256]
        mtt = ssc[3][:, 256:384]
        rowa = ab[0:1, 1536:2048]
        rowb = a_s[0:1, 0:512]
        rowc = a_s[0:1, 512:1024]
        bcm = [ab[:, 0:512], ab[:, 512:1024]]
        bcr = [ab[:, 1024:1536]] * 2
        gfb = ssc[2]
        gf2 = ssc[3]

        pp = [ctx.enter_context(nc.psum_tensor(f"pp{i}", [128, 1024], F32)) for i in range(4)]

        SM = {}
        for name in ["sx", "swA", "swB", "sw2A", "sw2B", "swv", "skvA", "skvB", "skv2",
                     "sbias", "srow", "smt", "scc", "sTV", "sRD", "sVS", "sSV", "sEV",
                     "sVG", "sGV", "sOUT", "sdq", "sST", "sdbg", "sfA", "sf2A",
                     "sccW", "sxio"]:
            SM[name] = Sem(nc, ctx, name)
        sx = SM["sx"]; swp = [SM["swA"], SM["swA"]]; sw2p = [SM["sw2A"], SM["sw2A"]]
        sfp = [SM["sfA"], SM["sfA"]]; sf2p = [SM["sf2A"], SM["sf2A"]]
        swv = SM["swv"]; skvp = [SM["skvA"], SM["skvB"]]; skv2 = SM["skv2"]
        sbias, srow, smt, scc = SM["sbias"], SM["srow"], SM["smt"], SM["scc"]
        sTV, sRD, sVS, sSV, sEV = SM["sTV"], SM["sRD"], SM["sVS"], SM["sSV"], SM["sEV"]
        sVG, sGV, sOUT, sdq, sST = SM["sVG"], SM["sGV"], SM["sOUT"], SM["sdq"], SM["sST"]
        sccW, sxio = SM["sccW"], SM["sxio"]
        ffn_ags = [[]]

        sdmp = SM["sdbg"]

        def dbg_dump(slot_i, src, bf16=False, w=512):
            if not (DBG.get("dump") or DBG.get("dump_ao") or DBG.get("dump_xt") or DBG.get("dump_xt2") or DBG.get("dump_wg")):
                return
            with nc.Block() as blk:
                @blk.sync
                def _(sy):
                    dst = dbg_t.ap()[slot_i]
                    if bf16:
                        dst = dst.rearrange("p q -> p q")[:, 0:(w + 1) // 2].bitcast(BF16)
                    else:
                        dst = dst[:, 0:w]
                    sdmp.dinc(sy.dma_start(out=dst, in_=src))
                    sdmp.wait(sy)

        rko = nc.sync.alloc_register("rko")
        rvo = nc.sync.alloc_register("rvo")
        rfill = nc.gpsimd.alloc_register("rfill")
        tab_pos = [0]
        bias_off = {"a": [0], "b": [0]}
        skc_off = {"a": [0], "b": [0]}

        def load_offsets(sy):
            i = tab_pos[0]
            sy.reg_load(rko, tab_sb[0:1, i:i + 1])
            sy.reg_load(rvo, tab_sb[0:1, i + 1:i + 2])
            tab_pos[0] += 2

        with nc.Block() as blk:
            @blk.sync
            def _(sy):
                sxio.dinc(sy.dma_start(out=idt[:, :], in_=idf))
                sxio.dinc(sy.dma_start(out=tab_sb[:, :], in_=tabt))
                if "a" in kinds:
                    sxio.dinc(sy.dma_start(out=qnw_sb[:, 0:cfg.dqa // 128],
                                         in_=qnw.rearrange("(c p) -> p c", p=128),
                                         allow_slow_non_contiguous=True))
                    sxio.dinc(sy.dma_start(out=knw_sb[:, 0:cfg.dka // 128],
                                         in_=knw.rearrange("(c p) -> p c", p=128),
                                         allow_slow_non_contiguous=True))

            @blk.vector
            def _(v):
                v.memset(onesf[:, :], 1.0)
                for j, val in enumerate([28, 22, 24, 18, -2147483648, 0x01C00000]):
                    v.memset(icol[:, j:j + 1], val)

            @blk.gpsimd
            def _(g):
                import struct as _st
                g.reg_mov(rfill, _st.unpack("<i", _st.pack("<f", NEG))[0])

        if has_ffn:
            dq_i2 = [ast[:, 0:512].bitcast(I32), ast[:, 512:1024].bitcast(I32)]
            dq_s2 = [ast[:, 1024:1536], ast[:, 1536:2048]]
            dq_t0 = ast[:, 2048:2560].bitcast(I32)
            dq_t1 = ast[:, 2560:3072].bitcast(I32)
            dq_b2 = [ast[:, 3072:3328].bitcast(BF16), ast[:, 3328:3584].bitcast(BF16)]

            items = []
            for src_p, src_s, nch, dst, dkind in [
                    (pg, sg, 1, wg_in, "gu"), (pu, su, 1, wu_in, "gu"),
                    (pdt, sdt, ndr // 128, wd_in, "d")]:
                nsl = src_p.shape[1] // 512
                for ch in range(nch):
                    for j in range(nsl):
                        items.append((src_p, src_s, ch, j, dst, dkind))
            kend = {}
            for m, (src_p, src_s, ch, j, dst, dkind) in enumerate(items):
                kend[id(dst)] = 2 * m + 2
            sx0, rd0, dq0, sv0 = sx.n, sRD.n, sdq.n, sVS.n

            def out_ap(dst, dkind, ch, j, hi):
                if dkind == "gu":
                    base = (4 * j) * 128 * 256 + hi * 128
                    return bass.AP(dst, base, [[256, 128], [128 * 256, 4], [1, 128]])
                ncl = 2 * ch + hi
                return bass.AP(dst, ncl * 128 + j * 4 * (128 * nclr * 128),
                               [[nclr * 128, 128], [128 * nclr * 128, 4], [1, 128]])

            with nc.Block() as blk:
                @blk.sync
                def _(sy):
                    for m, (src_p, src_s, ch, j, dst, dkind) in enumerate(items):
                        if m >= 2:
                            sy.wait_ge(sRD.h, rd0 + 2 * (m - 1))
                        sx.dinc(sy.dma_start(out=dq_i2[m % 2][:, :],
                                             in_=src_p[128 * ch:128 * ch + 128, 512 * j:512 * j + 512]))
                        sx.dinc(sy.dma_start(out=dq_s2[m % 2][:, :],
                                             in_=src_s[128 * ch:128 * ch + 128, 512 * j:512 * j + 512]))

                @blk.vector
                def _(v):
                    for m, (src_p, src_s, ch, j, dst, dkind) in enumerate(items):
                        v.wait_ge(sx.h, sx0 + 32 * (m + 1))
                        for hi in (0, 1):
                            k = 2 * m + hi
                            c1, c2 = (2, 3) if hi else (0, 1)
                            v.tensor_single_scalar(dq_t0[:, :], dq_i2[m % 2][:, :], icol[:, c1:c1 + 1], OP.logical_shift_left)
                            v.tensor_single_scalar(dq_t0[:, :], dq_t0[:, :], icol[:, 4:5], OP.bitwise_and)
                            v.tensor_single_scalar(dq_t1[:, :], dq_i2[m % 2][:, :], icol[:, c2:c2 + 1], OP.logical_shift_left)
                            v.tensor_single_scalar(dq_t1[:, :], dq_t1[:, :], icol[:, 5:6], OP.bitwise_and)
                            v.tensor_tensor(dq_t0[:, :], dq_t0[:, :], dq_t1[:, :], OP.bitwise_or)
                            if k >= 2:
                                v.wait_ge(sdq.h, dq0 + 16 * (k - 1))
                            sRD.inc(v.scalar_tensor_tensor(dq_b2[k % 2][:, :], dq_t0[:, :].bitcast(F32),
                                                           float(2.0 ** 126), dq_s2[m % 2][:, :],
                                                           OP.mult, OP.mult))

                @blk.gpsimd
                def _(g):
                    for m, (src_p, src_s, ch, j, dst, dkind) in enumerate(items):
                        for hi in (0, 1):
                            k = 2 * m + hi
                            g.wait_ge(sRD.h, rd0 + k + 1)
                            sdq.dinc(g.dma_start(out=out_ap(dst, dkind, ch, j, hi),
                                                 in_=dq_b2[k % 2][:, :].rearrange("p (t q) -> p t q", q=128)))
                    g.wait_ge(sdq.h, dq0 + 16 * 2 * len(items))
                    for ain, aout in [(wg_in, wg_full), (wu_in, wu_full), (wd_in, wd_full)]:
                        scc.inc(g.collective_compute("AllGather", OP.bypass,
                                                     replica_groups=[list(range(8))],
                                                     ins=[ain.ap().opt()], outs=[aout.ap().opt()]))
                    scc.wait(g)
            if DBG.get("dump_wfull"):
                wouts = []
                for nm, srct in [("wgo", wg_full), ("wuo", wu_full), ("wdo", wd_full)]:
                    o = nc.dram_tensor(nm, list(srct.shape), BF16, kind="ExternalOutput")
                    wouts.append((srct, o))
                with nc.Block() as blk:
                    @blk.sync
                    def _(sy):
                        for srct, o in wouts:
                            for r0 in range(0, srct.shape[0], 8192):
                                sdmp.dinc(sy.dma_start(out=o.ap()[r0:r0 + 8192, :],
                                                       in_=srct.ap()[r0:r0 + 8192, :]))
                        sdmp.wait(sy)
            if DBG.get("dump_wg"):
                for sl, srct in enumerate([wg_in, wu_in, wd_in]):
                    dbg_dump(sl, srct.ap()[128 * min(40, srct.shape[0] // 128 - 1):128 * min(40, srct.shape[0] // 128 - 1) + 128, 0:min(1024, srct.shape[1])], bf16=True, w=min(1024, srct.shape[1]))
                for sl, srct in enumerate([wg_full, wu_full, wd_full]):
                    dbg_dump(3 + sl, srct.ap()[128 * min(40, srct.shape[0] // 128 - 1):128 * min(40, srct.shape[0] // 128 - 1) + 128, 0:min(1024, srct.shape[1])], bf16=True, w=min(1024, srct.shape[1]))

        def bounce_rows(rows_dsts):
            sl = slot["ln"]
            slot["ln"] += 1
            assert sl < 96
            with nc.Block() as blk:
                @blk.sync
                def _(sy):
                    for i, (row, _) in enumerate(rows_dsts):
                        srow.dinc(sy.dma_start(out=bass.AP(lnrow, sl * 1024 + 512 * i, [[512, 1], [1, 512]]),
                                               in_=row[:, :]))
                    srow.wait(sy)
                    for i, (_, dst) in enumerate(rows_dsts):
                        srow.dinc(sy.dma_start(out=dst[:, :],
                                               in_=bass.AP(lnrow, sl * 1024 + 512 * i, [[0, 128], [1, 512]])))

        def ln_stage():
            for th in range(nth):
                t0c = 512 * th
                with nc.Block() as blk:
                    @blk.tensor
                    def _(t):
                        for c in range(dc):
                            t.matmul(pp[3][0:1, 0:512], onesf[:, :],
                                     XT[:, c * tpc + t0c: c * tpc + t0c + 512],
                                     start=(c == 0), stop=(c == dc - 1))
                base_sv, base_st = sSV.n, sST.n
                with nc.Block() as blk:
                    @blk.scalar
                    def _(s):
                        for c in range(dc):
                            if c >= 2:
                                s.wait_ge(sST.h, base_st + c - 1)
                            sSV.inc(s.activation(sqs[c % 2][:, :], XT[:, c * tpc + t0c: c * tpc + t0c + 512], AF.Square))

                    @blk.tensor
                    def _(t):
                        for c in range(dc):
                            t.wait_ge(sSV.h, base_sv + c + 1)
                            sST.inc(t.matmul(pp[3][0:1, 512:1024], onesf[:, :], sqs[c % 2][:, :],
                                             start=(c == 0), stop=(c == dc - 1)))
                with nc.Block() as blk:
                    @blk.vector
                    def _(v):
                        v.tensor_scalar(rowa[:, :], pp[3][0:1, 0:512], 1.0 / cfg.d, None, OP.mult)
                        v.tensor_tensor(rowc[:, :], rowa[:, :], rowa[:, :], OP.mult)
                        v.scalar_tensor_tensor(rowb[:, :], pp[3][0:1, 512:1024], 1.0 / cfg.d, rowc[:, :], OP.mult, OP.subtract)
                        v.tensor_scalar(rowb[:, :], rowb[:, :], 1e-5, None, OP.add)
                with nc.Block() as blk:
                    @blk.scalar
                    def _(s):
                        s.activation(rowb[:, :], rowb[:, :], AF.Sqrt)
                with nc.Block() as blk:
                    @blk.vector
                    def _(v):
                        v.reciprocal(rowb[:, :], rowb[:, :])
                bounce_rows([(rowa, bcm[th % 2]), (rowb, bcr[th % 2])])
                with nc.Block() as blk:
                    @blk.vector
                    def _(v):
                        srow.wait(v)
                        for c in range(dc):
                            cs = slice(c * tpc + t0c, c * tpc + t0c + 512)
                            v.tensor_tensor(ssc[0][:, :], XT[:, cs], bcm[th % 2][:, :], OP.subtract)
                            v.tensor_tensor(XN[:, cs], ssc[0][:, :], bcr[th % 2][:, :], OP.mult)

        def projT(srcw, njt, dst, scale, rms, t0=0, tw=None, dstw=None):
            tw = tpc if tw is None else tw
            dstw = tw if dstw is None else dstw
            nh = tw // 512
            w0, tv0, rd0, sv0, st0 = swp[0].n, sTV.n, sRD.n, sVS.n, sST.n
            bufs = [_w0, _w1]
            sq4 = [ssc[0], ssc[1], ssc[2], ssc[3]]
            nk = njt * nh

            with nc.Block() as blk:
                @blk.gpsimd
                def _(g):
                    for j in range(njt):
                        if j >= 2:
                            g.wait_ge(sTV.h, tv0 + j - 1)
                        swp[0].dinc(g.dma_start(out=bufs[j % 2][:, 0:dc * 128], in_=srcw[j]))

                @blk.tensor
                def _(t):
                    def ones_mm(j):
                        for th in range(nh):
                            k = j * nh + th
                            t.wait_ge(sVS.h, sv0 + k + 1)
                            sST.inc(t.matmul(pp[3][0:1, 512 * th:512 * th + 512], onesf[:, :],
                                             sq4[k % 4][:, :],
                                             start=(j == 0), stop=(j == njt - 1)))
                    for j in range(njt):
                        t.wait_ge(swp[0].h, w0 + 16 * (j + 1))
                        if j >= 2:
                            t.wait_ge(sRD.h, rd0 + j - 1)
                            if rms:
                                t.wait_ge(sVS.h, sv0 + (j - 1) * nh)
                        ins = None
                        for c in range(dc):
                            for th in range(nh):
                                ins = t.matmul(pp[j % 2][:, 512 * th:512 * th + 512],
                                               bufs[j % 2][:, c * 128:c * 128 + 128],
                                               XN[:, c * tpc + t0 + 512 * th: c * tpc + t0 + 512 * th + 512],
                                               start=(c == 0), stop=(c == dc - 1))
                        sTV.inc(ins)
                        if rms and j >= 1:
                            ones_mm(j - 1)
                    if rms:
                        ones_mm(njt - 1)

                @blk.vector
                def _(v):
                    for j in range(njt):
                        v.wait_ge(sTV.h, tv0 + j + 1)
                        if scale is not None:
                            ins = v.tensor_scalar(dst[:, j * dstw:j * dstw + tw], pp[j % 2][:, 0:tw],
                                                  float(scale), None, OP.mult)
                        else:
                            ins = v.tensor_copy(dst[:, j * dstw:j * dstw + tw], pp[j % 2][:, 0:tw])
                        sRD.inc(ins)

                if rms:
                    @blk.scalar
                    def _(s):
                        for j in range(njt):
                            s.wait_ge(sTV.h, tv0 + j + 1)
                            s.wait_ge(sRD.h, rd0 + j + 1)
                            for th in range(nh):
                                k = j * nh + th
                                if k >= 4:
                                    s.wait_ge(sST.h, st0 + k - 3)
                                sVS.inc(s.activation(sq4[k % 4][:, :],
                                                     pp[j % 2][:, 512 * th:512 * th + 512], AF.Square))

        def rms_apply(dst, njt, ndim, eps, extra, wcols, dstw=None, nh=None):
            dstw = tpc if dstw is None else dstw
            nh = nth if nh is None else nh
            for th in range(nh):
                with nc.Block() as blk:
                    @blk.vector
                    def _(v):
                        v.tensor_scalar(rowa[:, :], pp[3][0:1, 512 * th:512 * th + 512],
                                        1.0 / ndim, float(eps), OP.mult, OP.add)
                with nc.Block() as blk:
                    @blk.scalar
                    def _(s):
                        s.activation(rowa[:, :], rowa[:, :], AF.Sqrt)
                with nc.Block() as blk:
                    @blk.vector
                    def _(v):
                        v.reciprocal(rowa[:, :], rowa[:, :])
                        if extra != 1.0:
                            v.tensor_scalar(rowa[:, :], rowa[:, :], float(extra), None, OP.mult)
                bounce_rows([(rowa, bcm[th % 2])])
            with nc.Block() as blk:
                @blk.vector
                def _(v):
                    srow.wait(v)
                    for j in range(njt):
                        for th in range(nh):
                            cs = slice(j * dstw + 512 * th, j * dstw + 512 * th + 512)
                            if wcols is not None:
                                v.scalar_tensor_tensor(dst[:, cs], dst[:, cs], wcols[:, j:j + 1],
                                                       bcm[th % 2][:, :], OP.mult, OP.mult)
                            else:
                                v.tensor_tensor(dst[:, cs], dst[:, cs], bcm[th % 2][:, :], OP.mult)

        def dma_out_rows(dst_dram, src_sb, nkt):
            with nc.Block() as blk:
                @blk.sync
                def _(sy):
                    for j in range(nkt):
                        skv2.dinc(sy.dma_start(out=dst_dram.ap()[128 * j:128 * j + 128, :],
                                               in_=src_sb[:, j * tpc:(j + 1) * tpc]))

        def proj_v(srcw, ndvs, dst_dram):
            for dvs in range(ndvs):
                tv0, rd0, kv0 = sTV.n, sRD.n, skv2.n
                with nc.Block() as blk:
                    @blk.gpsimd
                    def _(g):
                        swv.dinc(g.dma_start(out=wvsl[:, 0:dc * 512], in_=srcw[dvs]))

                    @blk.tensor
                    def _(t):
                        swv.wait(t)
                        for tt in range(ntt):
                            if tt >= 2:
                                t.wait_ge(sRD.h, rd0 + tt - 1)
                            ins = None
                            for c in range(dc):
                                ins = t.matmul(pp[tt % 2][:, 0:512],
                                               XN[:, c * tpc + 128 * tt: c * tpc + 128 * tt + 128],
                                               wvsl[:, c * 512:c * 512 + 512],
                                               start=(c == 0), stop=(c == dc - 1))
                            sTV.inc(ins)

                    @blk.vector
                    def _(v):
                        for tt in range(ntt):
                            v.wait_ge(sTV.h, tv0 + tt + 1)
                            if tt >= 2:
                                v.wait_ge(skv2.h, kv0 + 16 * (tt - 1))
                            sRD.inc(v.tensor_copy(kvstg[tt % 2][:, 0:512], pp[tt % 2][:, 0:512]))

                    @blk.sync
                    def _(sy):
                        for tt in range(ntt):
                            sy.wait_ge(sRD.h, rd0 + tt + 1)
                            skv2.dinc(sy.dma_start(
                                out=dst_dram.ap()[128 * tt:128 * tt + 128, 512 * dvs:512 * dvs + 512],
                                in_=kvstg[tt % 2][:, 0:512]))

        def oproj(srcw, ndvc, qg):
            qgl = 512 * qg
            w0, tv0, rd0 = swp[0].n, sTV.n, sRD.n
            bufs = [_w0, _w1]
            with nc.Block() as blk:
                @blk.gpsimd
                def _(g):
                    for j in range(dc):
                        if j >= 2:
                            g.wait_ge(sTV.h, tv0 + j - 1)
                        swp[0].dinc(g.dma_start(out=bufs[j % 2][:, 0:ndvc * 128], in_=srcw[j]))

                @blk.tensor
                def _(t):
                    for j in range(dc):
                        t.wait_ge(swp[0].h, w0 + 16 * (j + 1))
                        if j >= 2:
                            t.wait_ge(sRD.h, rd0 + j - 1)
                        ins = None
                        for c in range(ndvc):
                            ins = t.matmul(pp[j % 2][:, 0:512], bufs[j % 2][:, c * 128:c * 128 + 128],
                                           AO[:, c * 512:c * 512 + 512],
                                           start=(c == 0), stop=(c == ndvc - 1))
                        sTV.inc(ins)

                @blk.vector
                def _(v):
                    for j in range(dc):
                        v.wait_ge(sTV.h, tv0 + j + 1)
                        cs = slice(j * tpc + qgl, j * tpc + qgl + 512)
                        sRD.inc(v.tensor_tensor(XT[:, cs], XT[:, cs], pp[j % 2][:, 0:512], OP.add))

        def attention(kind, pidx, ow, ndvc, qsrc):
            isa = kind == "a"
            H = cfg.ha if isa else cfg.hb
            DH = cfg.dha if isa else cfg.dhb
            KVH = cfg.kva if isa else cfg.hb
            rep = H // KVH
            dk = KVH * DH
            nkch = dk // 128
            kk = kv[kind]
            wmax = kk["wmax"]
            bra_t = bias_in["bra" if isa else "brb"]
            sca_t = bias_in["sca" if isa else "scb"]
            HG = cfg.hg
            if isa:
                projT(qsrc, cfg.dqa // 128, QT, None, True)
                rms_apply(QT, cfg.dqa // 128, cfg.dqa, 1e-6, 1.0 / math.sqrt(cfg.dha), qnw_sb)
            else:
                projT(qsrc, cfg.dvb // 128, QT, 1.0 / math.sqrt(cfg.dhb), False)
            for qg in range(nqg):
                kblk, nkb = kblocks(cfg, kind, pidx, qg)
                ncpb = kblk // 128
                W = nkb * kblk
                qgl = 512 * qg
                relk = [512 - (nkb - i) * kblk for i in range(nkb)]
                boff0 = bias_off[kind][0]
                soff0 = skc_off[kind][0]
                with nc.Block() as blk:
                    @blk.sync
                    def _(sy):
                        sEV.wait(sy)
                        load_offsets(sy)
                        skv2.dinc(sy.dma_start(
                            out=bass.AP(kk["kwin"], 0, [[wmax, dk], [1, W]]),
                            in_=bass.AP(kk["kg"], rko, [[PAD + cfg.s, dk], [1, W]])))
                        skv2.dinc(sy.dma_start(
                            out=bass.AP(kk["vwin"], 0, [[dk, W], [1, dk]]),
                            in_=bass.AP(kk["vg"], rvo, [[dk, W], [1, dk]])))
                dbg_here = (DBG.get("dump") and kind == DBG.get("kind")
                            and pidx == DBG.get("pidx") and qg == DBG.get("qg"))
                if dbg_here:
                    dbg_dump(0, QT[:, 0:512], bf16=True)
                    dbg_dump(1, QT[:, 512:1024], bf16=True)
                with nc.Block() as blk:
                    @blk.vector
                    def _(v):
                        sEV.wait(v)
                        v.memset(mxp[:, 0:H * 4 * nkb], NEG)

                    @blk.sync
                    def _(sy):
                        skv2.wait(sy)

                tiles = []
                for kbi in range(nkb):
                    for h in range(H):
                        for qt in range(4):
                            r0 = relk[kbi] - 128 * qt
                            up = (r0 + kblk - 1) > 0
                            wn = isa and (r0 < -cfg.win + 127)
                            tiles.append((kbi, h, qt, up or wn))
                tpkb = H * 4
                tv0, rd0, sv0 = sTV.n, sRD.n, sSV.n
                gv0, vg0 = sGV.n, sVG.n
                kv0 = [skvp[0].n, skvp[1].n]
                bias0 = sbias.n
                epre = []
                ec = 0
                for (_, _, _, e) in tiles:
                    epre.append(ec)
                    ec += 1 if e else 0

                with nc.Block() as blk:
                    @blk.sync
                    def _(sy):
                        for kbi in range(nkb):
                            if kbi >= 2:
                                sy.wait_ge(sTV.h, tv0 + (kbi - 1) * tpkb)
                            if isa:
                                for rr in (0, 1):
                                    skvp[kbi % 2].dinc(sy.dma_start(
                                        out=kst[kbi % 2][64 * rr:64 * rr + 64, 0:KVH * kblk]
                                        .rearrange("p (c k) -> p c k", k=kblk),
                                        in_=bass.AP(kk["kwin"], kbi * kblk,
                                                    [[wmax, 64], [64 * wmax, KVH], [1, kblk]])))
                            else:
                                skvp[kbi % 2].dinc(sy.dma_start(
                                    out=kst[kbi % 2][:, 0:nkch * kblk].rearrange("p (c k) -> p c k", k=kblk),
                                    in_=bass.AP(kk["kwin"], kbi * kblk, [[wmax, 128], [128 * wmax, nkch], [1, kblk]])))
                            for h in range(H):
                                if h >= 4:
                                    sy.wait_ge(sRD.h, rd0 + kbi * tpkb + (h - 4) * 4 + 4)
                                elif kbi >= 1:
                                    sy.wait_ge(sRD.h, rd0 + (kbi - 1) * tpkb + (H - 4 + h) * 4 + 4)
                                sbias.dinc(sy.dma_start(
                                    out=bias2[h % 4][:, 0:kblk],
                                    in_=bass.AP(bra_t.tensor, boff0 + kbi * H * kblk + h * kblk,
                                                [[0, 128], [1, kblk]])))

                    @blk.tensor
                    def _(t):
                        for j, (kbi, h, qt, edge) in enumerate(tiles):
                            sl = j % 4
                            bank = pp[2 + sl // 2][:, 512 * (sl % 2):512 * (sl % 2) + 512]
                            kd = 2 if isa else 1
                            t.wait_ge(skvp[kbi % 2].h, kv0[kbi % 2] + 16 * kd * (kbi // 2 + 1))
                            if j >= 4:
                                t.wait_ge(sRD.h, rd0 + j - 3)
                            kvh = h // rep
                            qrow = (h * DH) // 128
                            qoff = (h * DH) % 128
                            cch = kvh if isa else (kvh * DH) // 128
                            coff = qoff if isa else (kvh * DH) % 128
                            sTV.inc(t.matmul(
                                bank[:, 0:kblk],
                                QT[qoff:qoff + DH, qrow * tpc + qgl + 128 * qt: qrow * tpc + qgl + 128 * qt + 128],
                                kst[kbi % 2][coff:coff + DH, cch * kblk: cch * kblk + kblk],
                                start=True, stop=True))

                    @blk.vector
                    def _(v):
                        for j, (kbi, h, qt, edge) in enumerate(tiles):
                            sl = j % 4
                            bank = pp[2 + sl // 2][:, 512 * (sl % 2):512 * (sl % 2) + 512]
                            v.wait_ge(sTV.h, tv0 + j + 1)
                            v.wait_ge(sbias.h, bias0 + 16 * (kbi * H + h + 1))
                            if j >= 4:
                                v.wait_ge(sSV.h, sv0 + j - 3)
                            colidx = (h * 4 + qt) * nkb + kbi
                            if isa:
                                ins = v.tensor_tensor(ssc[sl][:, 0:kblk], bank[:, 0:kblk], bias2[h % 4][:, 0:kblk], OP.add)
                            else:
                                v.tensor_scalar(ssc[sl][:, 0:kblk], bank[:, 0:kblk], 1024.0, -1024.0, OP.min, OP.max)
                                ins = v.tensor_tensor(ssc[sl][:, 0:kblk], ssc[sl][:, 0:kblk], bias2[h % 4][:, 0:kblk], OP.add)
                            if edge:
                                sVG.inc(ins)
                                v.wait_ge(sGV.h, gv0 + epre[j] + 1)
                            sRD.inc(v.reduce_max(mxp[:, colidx:colidx + 1], ssc[sl][:, 0:kblk], AX.X, negate=True))

                    @blk.gpsimd
                    def _(g):
                        for j, (kbi, h, qt, edge) in enumerate(tiles):
                            if not edge:
                                continue
                            sl = j % 4
                            r0 = relk[kbi] - 128 * qt
                            g.wait_ge(sVG.h, vg0 + epre[j] + 1)
                            ins = None
                            if (r0 + kblk - 1) > 0:
                                ins = g.affine_select(out=ssc[sl][:, 0:kblk], in_=ssc[sl][:, 0:kblk],
                                                      pattern=[[-1, kblk]], compare_op=OP.is_ge,
                                                      fill=rfill, base=-r0, channel_multiplier=1)
                            if isa and (r0 < -cfg.win + 127):
                                ins = g.affine_select(out=ssc[sl][:, 0:kblk], in_=ssc[sl][:, 0:kblk],
                                                      pattern=[[1, kblk]], compare_op=OP.is_ge,
                                                      fill=rfill, base=r0 + cfg.win, channel_multiplier=-1)
                            sGV.inc(ins)

                    @blk.scalar
                    def _(s):
                        for j, (kbi, h, qt, edge) in enumerate(tiles):
                            sl = j % 4
                            colidx = (h * 4 + qt) * nkb + kbi
                            s.wait_ge(sRD.h, rd0 + j + 1)
                            sSV.inc(s.activation(scrapb[:, 0:kblk], ssc[sl][:, 0:kblk], AF.Exp,
                                                 bias=mxp[:, colidx:colidx + 1],
                                                 accum_out=psp[:, colidx:colidx + 1]))

                if dbg_here:
                    dbg_dump(2, mxp[:, 0:512])
                    dbg_dump(3, psp[:, 0:512])
                    dbg_dump(4, kst[(nkb - 1) % 2][:, 0:512], bf16=True)
                with nc.Block() as blk:
                    @blk.vector
                    def _(v):
                        for hq in range(H * 4):
                            v.tensor_reduce(nmb[:, hq:hq + 1], mxp[:, hq * nkb:(hq + 1) * nkb], AX.X, OP.min)
                        for hq in range(H * 4):
                            v.tensor_scalar(mxp[:, hq * nkb:(hq + 1) * nkb], mxp[:, hq * nkb:(hq + 1) * nkb],
                                            nmb[:, hq:hq + 1], None, OP.subtract)
                with nc.Block() as blk:
                    @blk.scalar
                    def _(s):
                        s.activation(mxp[:, 0:H * 4 * nkb], mxp[:, 0:H * 4 * nkb], AF.Exp, scale=-1.0)
                with nc.Block() as blk:
                    @blk.vector
                    def _(v):
                        v.tensor_tensor(psp[:, 0:H * 4 * nkb], psp[:, 0:H * 4 * nkb], mxp[:, 0:H * 4 * nkb], OP.mult)
                        for hq in range(H * 4):
                            ins = v.reduce_sum(ldb[:, hq:hq + 1], psp[:, hq * nkb:(hq + 1) * nkb], AX.X)
                        sRD.inc(ins)
                with nc.Block() as blk:
                    @blk.scalar
                    def _(s):
                        s.activation(ldb[:, 0:H * 4], ldb[:, 0:H * 4], AF.Ln)
                with nc.Block() as blk:
                    @blk.vector
                    def _(v):
                        v.tensor_tensor(nmb[:, 0:H * 4], nmb[:, 0:H * 4], ldb[:, 0:H * 4], OP.subtract)
                        if H * 4 < 128:
                            v.memset(nmb[:, H * 4:128], 0.0)
                with nc.Block() as blk:
                    @blk.tensor
                    def _(t):
                        sTV.inc(t.transpose(pp[3][:, 512:640], nmb[:, :], idt[:, :]))

                    @blk.vector
                    def _(v):
                        sTV.wait(v)
                        v.tensor_copy(mtt[:, :], pp[3][:, 512:640])
                msl = slot["mt"]
                slot["mt"] += 1
                assert msl < 48
                with nc.Block() as blk:
                    @blk.sync
                    def _(sy):
                        smt.dinc(sy.dma_start(out=mtd.ap()[msl], in_=mtt[:, :]))
                        smt.wait(sy)
                if dbg_here:
                    dbg_dump(5, ssc[3][:, 0:512])

                nhg = (H + HG - 1) // HG
                rdF = sRD.n
                evq, kvq = sEV.n, [skvp[0].n, skvp[1].n]
                for hgi in range(nhg):
                    heads = list(range(hgi * HG, min(H, hgi * HG + HG)))
                    nh = len(heads)
                    with nc.Block() as blk:
                        @blk.sync
                        def _(sy):
                            for i, h in enumerate(heads):
                                smt.dinc(sy.dma_start(
                                    out=nmld[i][:, :],
                                    in_=bass.AP(mtd, msl * 16384 + (h * 4) * 128, [[0, 128], [1, 512]])))
                    tiles2 = []
                    for kbi in range(nkb):
                        for kc in range(ncpb):
                            for i in range(nh):
                                tiles2.append((kbi, kc, i))
                    n2 = len(tiles2)
                    first_of = {}
                    last_of = {}
                    for j, (kbi, kc, i) in enumerate(tiles2):
                        if i not in first_of:
                            first_of[i] = j
                        last_of[i] = j
                    tpkb2 = ncpb * nh
                    tv0, rd0, vs0, sv0, ev0 = sTV.n, sRD.n, sVS.n, sSV.n, sEV.n
                    gv0, vg0 = sGV.n, sVG.n
                    kv0 = [skvp[0].n, skvp[1].n]
                    smt0 = smt.n
                    e2 = []
                    nv_at = []
                    ne_at = []
                    nv = ne = 0
                    for (kbi, kc, i) in tiles2:
                        r0 = relk[kbi] + 128 * kc
                        e = ((r0 + 127) > 0) or (isa and (r0 < -cfg.win + 511))
                        if e:
                            ne += 1
                        else:
                            nv += 1
                        e2.append((e, ne - 1 if e else 0))
                        nv_at.append(nv)
                        ne_at.append(ne)
                    LAG = 3

                    dbg_on = (DBG.get("atb") and isa and pidx == 0 and qg == 0 and hgi == 0)
                    sdbg = SM["sdbg"]
                    with nc.Block() as blk:
                        @blk.sync
                        def _(sy):
                            if dbg_on:
                                sy.wait_ge(smt.h, smt0)
                                sdbg.dinc(sy.dma_start(
                                    out=dbg_atb.ap()[10].bitcast(mybir.dt.float32)[:, 0:256],
                                    in_=nmld[0][:, 0:256]))
                            for kbi in range(nkb):
                                if dbg_on and kbi == 2:
                                    sy.wait_ge(sVS.h, vs0 + nv_at[4])
                                    sdbg.dinc(sy.dma_start(
                                        out=dbg_atb.ap()[12].bitcast(mybir.dt.float32)[:, 0:256],
                                        in_=ssc[0][:, 0:256]))
                                if dbg_on and kbi >= 2:
                                    for m in range((kbi - 2) * ncpb, (kbi - 1) * ncpb):
                                        sy.wait_ge(sSV.h, sv0 + 4 * m + 1)
                                        sdbg.dinc(sy.dma_start(out=dbg_atb.ap()[m],
                                                               in_=atb[0][:, :]))
                                if kbi >= 2:
                                    sy.wait_ge(sEV.h, ev0 + (kbi - 1) * tpkb2)
                                if isa:
                                    for rr in (0, 1):
                                        skvp[kbi % 2].dinc(sy.dma_start(
                                            out=kst[kbi % 2][64 * rr:64 * rr + 64, 0:KVH * kblk]
                                            .rearrange("p (c k) -> p c k", k=kblk),
                                            in_=bass.AP(kk["kwin"], kbi * kblk,
                                                        [[wmax, 64], [64 * wmax, KVH], [1, kblk]])))
                                else:
                                    skvp[kbi % 2].dinc(sy.dma_start(
                                        out=kst[kbi % 2][:, 0:nkch * kblk].rearrange("p (c k) -> p c k", k=kblk),
                                        in_=bass.AP(kk["kwin"], kbi * kblk, [[wmax, 128], [128 * wmax, nkch], [1, kblk]])))
                                skvp[kbi % 2].dinc(sy.dma_start(
                                    out=vst[kbi % 2][:, 0:ncpb * dk].rearrange("p (c d) -> p c d", d=dk),
                                    in_=bass.AP(kk["vwin"], kbi * kblk * dk, [[dk, 128], [128 * dk, ncpb], [1, dk]])))
                                skvp[kbi % 2].dinc(sy.dma_start(
                                    out=skcb[:, 64 * (kbi % 2):64 * (kbi % 2) + H * ncpb],
                                    in_=bass.AP(sca_t.tensor, soff0 + kbi * H * ncpb * 128,
                                                [[H * ncpb, 128], [1, H * ncpb]])))
                            if dbg_on:
                                for m in range((nkb - 2) * ncpb, nkb * ncpb):
                                    sy.wait_ge(sSV.h, sv0 + 4 * m + 1)
                                    sdbg.dinc(sy.dma_start(out=dbg_atb.ap()[m],
                                                           in_=atb[0][:, :]))

                        @blk.tensor
                        def _(t):
                            def av(jj):
                                kbi_, kc_, i_ = tiles2[jj]
                                h_ = heads[i_]
                                kvh_ = h_ // rep
                                avb = pp[i_ // 2][:, 512 * (i_ % 2):512 * (i_ % 2) + 512]
                                t.wait_ge(sSV.h, sv0 + jj + 1)
                                sEV.inc(t.matmul(
                                    avb[0:DH, :],
                                    vst[kbi_ % 2][:, kc_ * dk + kvh_ * DH: kc_ * dk + kvh_ * DH + DH],
                                    atb[jj % 4][:, :],
                                    start=(jj == first_of[i_]), stop=(jj == last_of[i_])))
                            for j, (kbi, kc, i) in enumerate(tiles2):
                                h = heads[i]
                                kvh = h // rep
                                qrow = (h * DH) // 128
                                qoff = (h * DH) % 128
                                cch = kvh if isa else (kvh * DH) // 128
                                coff = qoff if isa else (kvh * DH) % 128
                                sl = j % 4
                                bank = pp[2 + sl // 2][:, 512 * (sl % 2):512 * (sl % 2) + 512]
                                kd2 = (2 if isa else 1) + 2
                                t.wait_ge(skvp[kbi % 2].h, kv0[kbi % 2] + 16 * kd2 * (kbi // 2 + 1))
                                if j >= 4:
                                    jj4 = j - 4
                                    if e2[jj4][0]:
                                        t.wait_ge(sVG.h, vg0 + ne_at[jj4])
                                    else:
                                        t.wait_ge(sVS.h, vs0 + nv_at[jj4])
                                sTV.inc(t.matmul(
                                    bank,
                                    kst[kbi % 2][coff:coff + DH, cch * kblk + 128 * kc: cch * kblk + 128 * kc + 128],
                                    QT[qoff:qoff + DH, qrow * tpc + qgl: qrow * tpc + qgl + 512],
                                    start=True, stop=True))
                                if j >= LAG:
                                    av(j - LAG)
                            for jj in range(max(0, n2 - LAG), n2):
                                av(jj)

                        @blk.vector
                        def _(v):
                            for j, (kbi, kc, i) in enumerate(tiles2):
                                h = heads[i]
                                sl = j % 4
                                bank = pp[2 + sl // 2][:, 512 * (sl % 2):512 * (sl % 2) + 512]
                                v.wait_ge(sTV.h, tv0 + j + 1)
                                v.wait_ge(skvp[kbi % 2].h, kv0[kbi % 2] + 16 * ((2 if isa else 1) + 2) * (kbi // 2 + 1))
                                v.wait_ge(smt.h, smt0)
                                if j >= 4:
                                    v.wait_ge(sSV.h, sv0 + j - 3)
                                kcolap = skcb[:, 64 * (kbi % 2) + h * ncpb + kc:
                                              64 * (kbi % 2) + h * ncpb + kc + 1]
                                if isa:
                                    ins = v.scalar_tensor_tensor(ssc[sl][:, :], bank, kcolap,
                                                                 nmld[i][:, :], OP.add, OP.add)
                                else:
                                    v.tensor_scalar(ssc[sl][:, :], bank, 1024.0, -1024.0, OP.min, OP.max)
                                    ins = v.scalar_tensor_tensor(ssc[sl][:, :], ssc[sl][:, :], kcolap,
                                                                 nmld[i][:, :], OP.add, OP.add)
                                if e2[j][0]:
                                    sVG.inc(ins)
                                else:
                                    sVS.inc(ins)

                        @blk.gpsimd
                        def _(g):
                            for j, (kbi, kc, i) in enumerate(tiles2):
                                if not e2[j][0]:
                                    continue
                                sl = j % 4
                                r0 = relk[kbi] + 128 * kc
                                g.wait_ge(sVG.h, vg0 + ne_at[j])
                                ins = None
                                if (r0 + 127) > 0:
                                    ins = g.affine_select(out=ssc[sl][:, :], in_=ssc[sl][:, :],
                                                          pattern=[[1, 512]], compare_op=OP.is_ge,
                                                          fill=rfill, base=-r0, channel_multiplier=-1)
                                if isa and (r0 < -cfg.win + 511):
                                    ins = g.affine_select(out=ssc[sl][:, :], in_=ssc[sl][:, :],
                                                          pattern=[[-1, 512]], compare_op=OP.is_ge,
                                                          fill=rfill, base=r0 + cfg.win, channel_multiplier=1)
                                sGV.inc(ins)

                        @blk.scalar
                        def _(s):
                            for j, (kbi, kc, i) in enumerate(tiles2):
                                sl = j % 4
                                if e2[j][0]:
                                    s.wait_ge(sGV.h, gv0 + ne_at[j])
                                else:
                                    s.wait_ge(sVS.h, vs0 + nv_at[j])
                                if j >= 4:
                                    s.wait_ge(sEV.h, ev0 + j - 3)
                                if dbg_on and sl == 0 and j >= 4:
                                    s.wait_ge(sdbg.h, 16 * (j // 4 + 2))
                                sSV.inc(s.activation(atb[sl][:, :], ssc[sl][:, :], AF.Exp))

                    with nc.Block() as blk:
                        @blk.vector
                        def _(v):
                            for i, h in enumerate(heads):
                                avb = pp[i // 2][:, 512 * (i % 2):512 * (i % 2) + 512]
                                drow = (h * DH) // 128
                                doff = (h * DH) % 128
                                v.tensor_copy(AO[doff:doff + DH, drow * 512:drow * 512 + 512], avb[0:DH, :])
                    if dbg_here and hgi == 0:
                        dbg_dump(6, nmld[0][:, :])
                        dbg_dump(7, vst[(nkb - 1) % 2][:, 0:512], bf16=True)
                        dbg_dump(8, AO[0:128, 0:512], bf16=True)
                        dbg_dump(9, atb[(len(tiles2) - 1) % 4][:, :], bf16=True)
                if DBG.get("dump_ao") and kind == DBG.get("kind") and pidx == DBG.get("pidx") and qg == DBG.get("qg"):
                    for c in range(min(16, ndvc)):
                        dbg_dump(c, AO[0:128, c * 512:(c + 1) * 512], bf16=True)
                bias_off[kind][0] += nkb * H * kblk
                skc_off[kind][0] += nkb * H * ncpb * 128
                oproj(ow, ndvc, qg)
            if DBG.get("dump_xt") and kind == DBG.get("kind") and pidx == DBG.get("pidx"):
                for c in range(dc):
                    dbg_dump(c, XT[:, c * tpc + DBG.get("xt_t0", 0):c * tpc + DBG.get("xt_t0", 0) + 512])

        def ffn():
            nblocks = dff // 2048
            ntt_g = nblocks * 16
            gbufs = [aw[:, 0:1024].bitcast(BF16), aw[:, 2048:3072].bitcast(BF16)]
            ubufs = [aw[:, 1024:2048].bitcast(BF16), aw[:, 3072:4096].bitcast(BF16)]
            dbufs = [ast[:, 0:1024].bitcast(BF16), ast[:, 1024:2048].bitcast(BF16)]
            fg0, fd0 = sfp[0].n, sf2p[0].n
            tg0, td0 = sTV.n, sST.n
            vg0, vd0 = sRD.n, sEV.n
            ga0, gb0 = sVS.n, sSV.n
            rpb = max(1, 16 // nclr)
            cpr = min(nclr, 16)

            with nc.Block() as blk:
                @blk.sync
                def _(sy):
                    for nt in range(ntt_g):
                        if nt >= 2:
                            sy.wait_ge(sTV.h, tg0 + 2 * (nt - 1))
                        src = bass.AP(wg_full, nt * 128 * 256, [[256, 128], [ngu_rows * 256, 8], [1, 256]])
                        sfp[0].dinc(sy.dma_start(out=gbufs[nt % 2][:, 0:2048].rearrange("p (r q) -> p r q", q=256), in_=src))
                        src2 = bass.AP(wu_full, nt * 128 * 256, [[256, 128], [ngu_rows * 256, 8], [1, 256]])
                        sfp[0].dinc(sy.dma_start(out=ubufs[nt % 2][:, 0:2048].rearrange("p (r q) -> p r q", q=256), in_=src2))

                @blk.gpsimd
                def _(g):
                    for nb in range(nblocks):
                        for dt in range(dc):
                            gdt = nb * dc + dt
                            if gdt >= 2:
                                g.wait_ge(sST.h, td0 + gdt - 1)
                            base = nb * (dc * 128 * nclr * 128) + dt * 128 * (nclr * 128)
                            src = bass.AP(wd_full, base,
                                          [[nclr * 128, 128], [dc * 128 * nclr * 128, rpb],
                                           [128, cpr], [1, 128]])
                            sf2p[0].dinc(g.dma_start(
                                out=dbufs[gdt % 2][:, 0:16 * 128].rearrange("p (r c q) -> p r c q", r=rpb, q=128),
                                in_=src))

                @blk.tensor
                def _(t):
                    for nt in range(ntt_g):
                        nb, ntl = nt // 16, nt % 16
                        ppg = pp[0 + 2 * (nt % 2)]
                        ppu = pp[1 + 2 * (nt % 2)]
                        t.wait_ge(sfp[0].h, fg0 + 32 * (nt + 1))
                        if nt >= 2:
                            t.wait_ge(sRD.h, vg0 + nt - 1)
                        if ntl == 1 and nb > 0:
                            t.wait_ge(sEV.h, vd0 + nb * dc)
                        gbuf, ubuf = gbufs[nt % 2], ubufs[nt % 2]
                        ins = None
                        for c in range(dc):
                            for th in range(nth):
                                ins = t.matmul(ppg[:, 512 * th:512 * th + 512], gbuf[:, c * 128:c * 128 + 128],
                                               XN[:, c * tpc + 512 * th: c * tpc + 512 * th + 512],
                                               start=(c == 0), stop=(c == dc - 1))
                        sTV.inc(ins)
                        for c in range(dc):
                            for th in range(nth):
                                ins = t.matmul(ppu[:, 512 * th:512 * th + 512], ubuf[:, c * 128:c * 128 + 128],
                                               XN[:, c * tpc + 512 * th: c * tpc + 512 * th + 512],
                                               start=(c == 0), stop=(c == dc - 1))
                        sTV.inc(ins)
                        if ntl == 15:
                            for dt in range(dc):
                                gdt = nb * dc + dt
                                ppd = pp[2 + gdt % 2]
                                t.wait_ge(sf2p[0].h, fd0 + 16 * (gdt + 1))
                                if dt == 0:
                                    t.wait_ge(sRD.h, vg0 + nt + 1)
                                if gdt >= 2:
                                    t.wait_ge(sEV.h, vd0 + gdt - 1)
                                dbuf = dbufs[gdt % 2]
                                for ncl in range(16):
                                    for th in range(nth):
                                        ins = t.matmul(ppd[:, 512 * th:512 * th + 512], dbuf[:, ncl * 128:ncl * 128 + 128],
                                                       IB[:, ncl * tpc + 512 * th: ncl * tpc + 512 * th + 512],
                                                       start=(ncl == 0), stop=(ncl == 15))
                                sST.inc(ins)

                @blk.vector
                def _(v):
                    for nt in range(ntt_g):
                        nb, ntl = nt // 16, nt % 16
                        ppg = pp[0 + 2 * (nt % 2)]
                        ppu = pp[1 + 2 * (nt % 2)]
                        v.wait_ge(sTV.h, tg0 + 2 * nt + 2)
                        for th in range(nth):
                            k = nt * nth + th
                            g_ps = ppg[:, 512 * th:512 * th + 512]
                            u_ps = ppu[:, 512 * th:512 * th + 512]
                            v.tensor_copy(gfb[:, :], g_ps)
                            v.tensor_tensor(gf2[:, :], gfb[:, :], gfb[:, :], OP.mult)
                            v.tensor_scalar(gf2[:, :], gf2[:, :], 0.044715, 1.0, OP.mult, OP.add)
                            sVS.inc(v.tensor_tensor(gf2[:, :], gf2[:, :], gfb[:, :], OP.mult))
                            v.wait_ge(sSV.h, gb0 + k + 1)
                            v.tensor_tensor(gf2[:, :], gf2[:, :], gfb[:, :], OP.mult)
                            ins = v.tensor_tensor(IB[:, ntl * tpc + 512 * th: ntl * tpc + 512 * th + 512],
                                                  gf2[:, :], u_ps, OP.mult)
                        sRD.inc(ins)
                        if ntl == 15:
                            for dt in range(dc):
                                gdt = nb * dc + dt
                                ppd = pp[2 + gdt % 2]
                                v.wait_ge(sST.h, td0 + gdt + 1)
                                sEV.inc(v.tensor_tensor(XT[:, dt * tpc:dt * tpc + tpc],
                                                        XT[:, dt * tpc:dt * tpc + tpc],
                                                        ppd[:, 0:tpc], OP.add))

                @blk.scalar
                def _(s):
                    for k in range(ntt_g * nth):
                        s.wait_ge(sVS.h, ga0 + k + 1)
                        sSV.inc(s.activation(gf2[:, :], gf2[:, :], AF.Sigmoid, scale=1.5957691216057308))

        def xio_in(pidx):
            tvX = sTV.n
            for tt in range(ntt):
                with nc.Block() as blk:
                    @blk.sync
                    def _(sy):
                        if tt >= 2:
                            sy.wait_ge(sTV.h, tvX + (tt - 1) * dc)
                        sxio.dinc(sy.dma_start(out=stg[tt % 2][:, :], in_=xin[pidx, 128 * tt:128 * tt + 128, :]))
                tv0, rd0 = sTV.n, sRD.n
                with nc.Block() as blk:
                    @blk.tensor
                    def _(t):
                        sxio.wait(t)
                        for c in range(dc):
                            if c >= 2:
                                t.wait_ge(sRD.h, rd0 + c - 1)
                            sTV.inc(t.transpose(pp[2 + c % 2][:, 512:640], stg[tt % 2][:, 128 * c:128 * c + 128], idt[:, :]))

                    @blk.vector
                    def _(v):
                        for c in range(dc):
                            v.wait_ge(sTV.h, tv0 + c + 1)
                            sRD.inc(v.tensor_copy(XT[:, c * tpc + 128 * tt: c * tpc + 128 * tt + 128],
                                                  pp[2 + c % 2][:, 512:640]))

        def xio_out(pidx):
            for tt in range(ntt):
                tv0, rd0 = sTV.n, sRD.n
                with nc.Block() as blk:
                    @blk.tensor
                    def _(t):
                        for c in range(dc):
                            if c >= 2:
                                t.wait_ge(sRD.h, rd0 + c - 1)
                            sTV.inc(t.transpose(pp[2 + c % 2][:, 512:640],
                                                XT[:, c * tpc + 128 * tt: c * tpc + 128 * tt + 128],
                                                idt[:, :]))

                    @blk.vector
                    def _(v):
                        for c in range(dc):
                            v.wait_ge(sTV.h, tv0 + c + 1)
                            sRD.inc(v.tensor_copy(stg[tt % 2][:, 128 * c:128 * c + 128], pp[2 + c % 2][:, 512:640]))
                with nc.Block() as blk:
                    @blk.sync
                    def _(sy):
                        sOUT.dinc(sy.dma_start(out=xout[pidx, 128 * tt:128 * tt + 128, :], in_=stg[tt % 2][:, :]))

        def dump_xt_at(tag, pidx):
            if DBG.get("dump_xt2") == tag and pidx == DBG.get("pidx"):
                for c in range(dc):
                    dbg_dump(c, XT[:, c * tpc + DBG.get("xt_t0", 0):c * tpc + DBG.get("xt_t0", 0) + 512])

        def attn_sublayer(kind, pidx):
            isa = kind == "a"
            dump_xt_at("pre_ln", pidx)
            ln_stage()
            dump_xt_at("post_ln", pidx)
            if isa:
                qsrc = [W["wqa"][j] for j in range(cfg.dqa // 128)]
                projT([W["wka"][j] for j in range(cfg.dka // 128)], cfg.dka // 128, IB, None, True)
                rms_apply(IB, cfg.dka // 128, cfg.dka, 1e-6, 1.0, knw_sb)
                dma_out_rows(kv["a"]["k_in"][pidx], IB, cfg.dka // 128)
                proj_v([W["wva"][j] for j in range(cfg.dka // 512)], cfg.dka // 512, kv["a"]["v_in"][pidx])
                ow = [W["woa"][j] for j in range(dc)]
            else:
                qsrc = [W["wqb"][j] for j in range(cfg.dvb // 128)]
                projT([W["wkb"][j] for j in range(cfg.dvb // 128)], cfg.dvb // 128, IB, None, False)
                dump_xt_at("post_kproj", pidx)
                dma_out_rows(kv["b"]["k_in"][pidx], IB, cfg.dvb // 128)
                dump_xt_at("post_krows", pidx)
                proj_v([W["wvb"][j] for j in range(cfg.dvb // 512)], cfg.dvb // 512, kv["b"]["v_in"][pidx])
                dump_xt_at("post_vproj", pidx)
                ow = [W["wob"][j] for j in range(dc)]
            kk = kv[kind]
            dk = cfg.dka if isa else cfg.dvb
            with nc.Block() as blk:
                @blk.gpsimd
                def _(g):
                    skv2.wait(g)
                    scc.inc(g.collective_compute("AllGather", OP.bypass,
                                                 replica_groups=[[0, 1], [2, 3], [4, 5], [6, 7]],
                                                 ins=[kk["k_in"][pidx].ap().opt()],
                                                 outs=[kk["k_out"].ap()[pidx].opt()]))
                    scc.inc(g.collective_compute("AllGather", OP.bypass,
                                                 replica_groups=[[0, 1], [2, 3], [4, 5], [6, 7]],
                                                 ins=[kk["v_in"][pidx].ap().opt()],
                                                 outs=[kk["v_out"].ap()[pidx].opt()]))
                    scc.wait(g)
            with nc.Block() as blk:
                @blk.sync
                def _(sy):
                    scc.wait(sy)
                    for rh in (0, 1):
                        for bi, j in enumerate(blocks_of(cfg, rh, pidx)):
                            pos = pidx * cfg.half + j * 512
                            lo = bi * 512
                            skv2.dinc(sy.dma_start(
                                out=bass.AP(kk["kg"], PAD + pos, [[PAD + cfg.s, dk], [1, 512]]),
                                in_=bass.AP(kk["k_out"], pidx * (2 * dk * tpc) + rh * dk * tpc + lo,
                                            [[tpc, dk], [1, 512]])))
                            skv2.dinc(sy.dma_start(
                                out=bass.AP(kk["vg"], (PAD + pos) * dk, [[dk, 512], [1, dk]]),
                                in_=bass.AP(kk["v_out"], pidx * (2 * tpc * dk) + (rh * tpc + lo) * dk,
                                            [[dk, 512], [1, dk]])))
                    skv2.wait(sy)
            dump_xt_at("pre_attn", pidx)
            attention(kind, pidx, ow, (cfg.dqa if isa else cfg.dvb) // 128, qsrc)

        with nc.Block() as blk:
            @blk.vector
            def _(v):
                v.memset(XN[:, 0:PAD], 0.0)
        with nc.Block() as blk:
            @blk.sync
            def _(sy):
                for kind in kinds:
                    dkz = cfg.dka if kind == "a" else cfg.dvb
                    for r0 in range(0, dkz, 128):
                        skv2.dinc(sy.dma_start(
                            out=bass.AP(kv[kind]["kg"], r0 * (PAD + cfg.s), [[PAD + cfg.s, 128], [1, PAD]]),
                            in_=XN[:, 0:PAD]))
                    for r0 in range(0, PAD, 128):
                        skv2.dinc(sy.dma_start(
                            out=bass.AP(kv[kind]["vg"], r0 * dkz, [[dkz, 128], [1, dkz]]),
                            in_=XN[:, 0:dkz]))
                skv2.wait(sy)

        for pidx in range(2):
            xio_in(pidx)
            for sl in cfg.sublayers:
                if sl == "attn_a":
                    attn_sublayer("a", pidx)
                elif sl == "attn_b":
                    attn_sublayer("b", pidx)
                elif sl == "ln":
                    ln_stage()
                    with nc.Block() as blk:
                        @blk.vector
                        def _(v):
                            for c in range(dc):
                                v.tensor_copy(XT[:, c * tpc:(c + 1) * tpc], XN[:, c * tpc:(c + 1) * tpc])
                else:
                    ln_stage()
                    ffn()
            xio_out(pidx)

        with nc.Block() as blk:
            @blk.sync
            def _(sy):
                sOUT.wait(sy)

    return nc


def run(cfg, inputs, trace=False):
    prep = host_prep(cfg, inputs)
    nc = build_program(cfg)
    kinds = active_kinds(cfg)
    in_maps = []
    for c in range(8):
        tb = prep["tab"][c]
        if len(tb) < 4:
            tb = np.concatenate([tb, np.zeros(4 - len(tb), np.int32)])
        m = {"xin": prep["xin"][c], "idf": prep["idf"], "tab": tb[None, :]}
        if "a" in kinds:
            m.update(wqa=prep["wqa"], wka=prep["wka"], wva=prep["wva"], woa=prep["woa"],
                     qnw=prep["qnw"], knw=prep["knw"], bra=prep["bra"][c], sca=prep["sca"][c])
        if "b" in kinds:
            m.update(wqb=prep["wqb"], wkb=prep["wkb"], wvb=prep["wvb"], wob=prep["wob"],
                     brb=prep["brb"][c], scb=prep["scb"][c])
        if "ffn" in cfg.sublayers:
            m.update(pg=prep["pg"][c], sg=prep["sg"][c], pu=prep["pu"][c], su=prep["su"][c],
                     pd=prep["pd"][c], sd=prep["sd"][c])
        in_maps.append(m)
    res = run_bass_kernel_spmd(nc, in_maps, core_ids=list(range(8)), trace=trace)
    s2 = prep["s2"]
    y_perm = np.zeros((cfg.b, cfg.s, cfg.d), np.float32)
    for c in range(8):
        seq = c // 2
        out = np.asarray(res.results[c]["xout"], np.float32)
        for p in range(2):
            bl = blocks_of(cfg, c % 2, p)
            for bi, j in enumerate(bl):
                y_perm[seq, p * cfg.half + j * 512: p * cfg.half + (j + 1) * 512] = \
                    out[p, bi * 512:(bi + 1) * 512]
    y = np.empty_like(y_perm)
    y[:, :, s2] = y_perm
    return y, res


def kernel(**inputs):
    cfg = Cfg()
    y, _ = run(cfg, inputs)
    return y


# version 12
# speedup vs baseline: 1.0505x; 1.0505x over previous
import math
from contextlib import ExitStack
from dataclasses import dataclass

import numpy as np
import ml_dtypes

import concourse.bass as bass
import concourse.mybir as mybir
from concourse.bass_utils import run_bass_kernel_spmd

F32 = mybir.dt.float32
BF16 = mybir.dt.bfloat16
I32 = mybir.dt.int32
AF = mybir.ActivationFunctionType
OP = mybir.AluOpType
AX = mybir.AxisListType
BF16NP = ml_dtypes.bfloat16
NEG = -1.0e30
DBG = {}


@dataclass
class Cfg:
    b: int = 4
    s: int = 4096
    d: int = 2048
    dff: int = 16384
    ha: int = 32
    kva: int = 8
    dha: int = 64
    win: int = 1024
    hb: int = 16
    dhb: int = 128
    sublayers: tuple = ("attn_a", "ffn", "attn_b", "ffn")
    hg: int = 4

    @property
    def half(self): return self.s // 2
    @property
    def tpc(self): return self.b * self.half // 8
    @property
    def dc(self): return self.d // 128
    @property
    def nqg(self): return self.tpc // 512
    @property
    def nblk(self): return self.half // 512
    @property
    def dka(self): return self.kva * self.dha
    @property
    def dqa(self): return self.ha * self.dha
    @property
    def dvb(self): return self.hb * self.dhb
    @property
    def nkcg(self): return self.s // 128


def sigma_perm(n):
    p = []
    for c in range(n // 256):
        p.extend(range(256 * c, 256 * c + 256, 2))
        p.extend(range(256 * c + 1, 256 * c + 256, 2))
    return np.array(p, dtype=np.int64)


def blocks_of(cfg, parity, p):
    nb = cfg.nblk
    k = cfg.tpc // 512
    if nb == 1:
        return [0]
    cand = [0, nb - 1, 2, nb - 3] if parity == 0 else [1, nb - 2, 3, nb - 4]
    return sorted(cand[:k])


def q0_of(cfg, parity, p, qg):
    return p * cfg.half + blocks_of(cfg, parity, p)[qg] * 512


def owner_of(cfg, pos):
    p = pos // cfg.half
    j = (pos % cfg.half) // 512
    for rh in (0, 1):
        bl = blocks_of(cfg, rh, p)
        if j in bl:
            return p, rh, bl.index(j) * 512 + (pos % 512)
    raise AssertionError(pos)


def kblocks(cfg, kind, p, qg):
    kblk = 256 if kind == "a" else 128
    nkb = 0
    for par in (0, 1):
        q0 = q0_of(cfg, par, p, qg)
        klo = (max(0, q0 - cfg.win) // kblk) * kblk if kind == "a" else 0
        nkb = max(nkb, (q0 + 512 - klo) // kblk)
    return kblk, nkb


def slopes_a(cfg):
    r = 2.0 ** (-8.0 / cfg.ha)
    return [r ** i for i in range(cfg.ha)]


def slopes_b(cfg):
    def _al(n):
        def pow2(m):
            start = 2.0 ** (-(2.0 ** (-(math.log2(m) - 3))))
            return [start * start ** i for i in range(m)]
        if math.log2(n).is_integer():
            return pow2(n)
        c = 2 ** math.floor(math.log2(n))
        return pow2(c) + _al(2 * c)[0::2][: n - c]
    return _al(cfg.hb)


def active_kinds(cfg):
    return [k[5] for k in cfg.sublayers if k.startswith("attn_")]


PAD = 4096


def iter_qgs(cfg):
    for pidx in range(2):
        for sl in cfg.sublayers:
            if sl.startswith("attn_"):
                kind = sl[5]
                for qg in range(cfg.nqg):
                    yield pidx, kind, qg


def build_tab(cfg, parity):
    tab = []
    for pidx, kind, qg in iter_qgs(cfg):
        dk = cfg.dka if kind == "a" else cfg.dvb
        q0 = q0_of(cfg, parity, pidx, qg)
        kblk, nkb = kblocks(cfg, kind, pidx, qg)
        W = nkb * kblk
        start = PAD + q0 + 512 - W
        tab.append(start)
        tab.append(start * dk)
    return np.asarray(tab, np.int32) if tab else np.zeros(2, np.int32)


def build_bias(cfg, parity):
    f = np.float32
    sa = np.array(slopes_a(cfg), f)
    sb = np.array(slopes_b(cfg), f)
    br = {"a": [], "b": []}
    sc = {"a": [], "b": []}
    for pidx, kind, qg in iter_qgs(cfg):
        H = cfg.ha if kind == "a" else cfg.hb
        sl = sa if kind == "a" else sb
        q0 = q0_of(cfg, parity, pidx, qg)
        kblk, nkb = kblocks(cfg, kind, pidx, qg)
        ncpb = kblk // 128
        for kbi in range(nkb):
            kb = q0 + 512 - (nkb - kbi) * kblk
            if kb < 0:
                br[kind].append(np.full((H, kblk), NEG, f))
                sc[kind].append(np.full((128, H * ncpb), NEG, f))
            else:
                pos = kb + np.arange(kblk, dtype=f)
                br[kind].append(sl[:, None] * pos[None, :])
                cpos = (kb + 128 * np.arange(ncpb, dtype=f)[None, :, None]
                        + np.arange(128, dtype=f)[:, None, None])
                sc[kind].append((sl[None, None, :] * cpos).transpose(0, 2, 1).reshape(128, H * ncpb))
    out = []
    for kind in ("a", "b"):
        out.append(np.concatenate([x.reshape(-1) for x in br[kind]]) if br[kind] else np.zeros(1, f))
        out.append(np.concatenate([x.reshape(-1) for x in sc[kind]]) if sc[kind] else np.zeros(1, f))
    return out


def tile4(wT, nj, ncc):
    K, M = wT.shape
    assert K == ncc * 128 and M == nj * 128, (wT.shape, nj, ncc)
    return np.ascontiguousarray(
        wT.reshape(ncc, 128, nj, 128).transpose(2, 1, 0, 3).astype(np.float32))


def vtile(wT, ndvs, ncc):
    return np.ascontiguousarray(
        wT.reshape(ncc, 128, ndvs, 512).transpose(2, 1, 0, 3).astype(np.float32))


def host_prep(cfg, inp):
    s2 = sigma_perm(cfg.d)
    s3 = sigma_perm(cfg.dff)
    f = np.float32
    prep = {"s2": s2}

    x = np.asarray(inp["L0_attn_a_x"], f)
    xp = np.ascontiguousarray(x[:, :, s2])
    xin = []
    for c in range(8):
        seq = c // 2
        passes = []
        for p in range(2):
            bl = blocks_of(cfg, c % 2, p)
            rows = [xp[seq, p * cfg.half + j * 512: p * cfg.half + (j + 1) * 512] for j in bl]
            passes.append(np.concatenate(rows, 0))
        xin.append(np.ascontiguousarray(np.stack(passes, 0)))
    prep["xin"] = xin

    dc = cfg.dc
    kinds = active_kinds(cfg)
    if "a" in kinds:
        prep["wqa"] = tile4(np.asarray(inp["L0_attn_a_wq"], f)[:, s2].T, cfg.dqa // 128, dc)
        prep["wka"] = tile4(np.asarray(inp["L0_attn_a_wk"], f)[:, s2].T, cfg.dka // 128, dc)
        prep["wva"] = vtile(np.asarray(inp["L0_attn_a_wv"], f)[:, s2].T, cfg.dka // 512, dc)
        prep["woa"] = tile4(np.ascontiguousarray(np.asarray(inp["L0_attn_a_wo"], f)[s2, :].T), dc, cfg.dqa // 128)
        prep["qnw"] = np.asarray(inp["L0_attn_a_q_norm_w"], f)
        prep["knw"] = np.asarray(inp["L0_attn_a_k_norm_w"], f)
    if "b" in kinds:
        prep["wqb"] = tile4(np.asarray(inp["L0_attn_b_Wq"], f)[:, s2].T, cfg.dvb // 128, dc)
        prep["wkb"] = tile4(np.asarray(inp["L0_attn_b_Wk"], f)[:, s2].T, cfg.dvb // 128, dc)
        prep["wvb"] = vtile(np.asarray(inp["L0_attn_b_Wv"], f)[:, s2].T, cfg.dvb // 512, dc)
        prep["wob"] = tile4(np.ascontiguousarray(np.asarray(inp["L0_attn_b_Wo"], f)[s2, :].T), dc, cfg.dvb // 128)
    if "ffn" in cfg.sublayers:
        def shard_gu(wp, sc):
            P_ = np.ascontiguousarray(np.asarray(wp).astype(np.int32).T[:, s3])
            S_ = np.ascontiguousarray(np.repeat(np.asarray(sc, f).T, 16, 0)[:, s3])
            return ([np.ascontiguousarray(P_[128 * c:128 * (c + 1)]) for c in range(8)],
                    [np.ascontiguousarray(S_[128 * c:128 * (c + 1)]) for c in range(8)])
        prep["pg"], prep["sg"] = shard_gu(inp["L0_ffn_gate_W_packed"], inp["L0_ffn_gate_scales"])
        prep["pu"], prep["su"] = shard_gu(inp["L0_ffn_up_W_packed"], inp["L0_ffn_up_scales"])
        Pd = np.ascontiguousarray(np.asarray(inp["L0_ffn_down_W_packed"]).astype(np.int32).T[:, s2])
        Sd = np.ascontiguousarray(np.repeat(np.asarray(inp["L0_ffn_down_scales"], f).T, 16, 0)[:, s2])
        nr = Pd.shape[0] // 8
        prep["pd"] = [np.ascontiguousarray(Pd[nr * c:nr * (c + 1)]) for c in range(8)]
        prep["sd"] = [np.ascontiguousarray(Sd[nr * c:nr * (c + 1)]) for c in range(8)]
    prep["idf"] = np.eye(128, dtype=f)
    prep["tab"] = [build_tab(cfg, c % 2) for c in range(8)]
    bb = [build_bias(cfg, c % 2) for c in range(8)]
    prep["bra"] = [b[0] for b in bb]
    prep["sca"] = [b[1] for b in bb]
    prep["brb"] = [b[2] for b in bb]
    prep["scb"] = [b[3] for b in bb]
    return prep


class Sem:
    def __init__(self, nc, ctx, name):
        self.h = ctx.enter_context(nc.semaphore(name))
        self.n = 0

    def inc(self, ins, k=1):
        ins.then_inc(self.h, k)
        self.n += k
        return self.n

    def dinc(self, ins):
        return self.inc(ins, 16)

    def wait(self, eng, v=None):
        eng.wait_ge(self.h, self.n if v is None else v)


def build_program(cfg):
    nc = bass.Bass("TRN2", target_bir_lowering=False, dynamic_dma_scratch_size=16384,
                   detect_race_conditions=False)
    dc, tpc, nqg, dff = cfg.dc, cfg.tpc, cfg.nqg, cfg.dff
    nth = tpc // 512
    ntt = tpc // 128
    kinds = active_kinds(cfg)
    has_ffn = "ffn" in cfg.sublayers

    xin = nc.dram_tensor("xin", [2, tpc, cfg.d], F32, kind="ExternalInput").ap()
    xout = nc.dram_tensor("xout", [2, tpc, cfg.d], F32, kind="ExternalOutput").ap()
    W = {}
    if "a" in kinds:
        W["wqa"] = nc.dram_tensor("wqa", [cfg.dqa // 128, 128, dc, 128], F32, kind="ExternalInput").ap()
        W["wka"] = nc.dram_tensor("wka", [cfg.dka // 128, 128, dc, 128], F32, kind="ExternalInput").ap()
        W["wva"] = nc.dram_tensor("wva", [cfg.dka // 512, 128, dc, 512], F32, kind="ExternalInput").ap()
        W["woa"] = nc.dram_tensor("woa", [dc, 128, cfg.dqa // 128, 128], F32, kind="ExternalInput").ap()
        qnw = nc.dram_tensor("qnw", [cfg.dqa], F32, kind="ExternalInput").ap()
        knw = nc.dram_tensor("knw", [cfg.dka], F32, kind="ExternalInput").ap()
    if "b" in kinds:
        W["wqb"] = nc.dram_tensor("wqb", [cfg.dvb // 128, 128, dc, 128], F32, kind="ExternalInput").ap()
        W["wkb"] = nc.dram_tensor("wkb", [cfg.dvb // 128, 128, dc, 128], F32, kind="ExternalInput").ap()
        W["wvb"] = nc.dram_tensor("wvb", [cfg.dvb // 512, 128, dc, 512], F32, kind="ExternalInput").ap()
        W["wob"] = nc.dram_tensor("wob", [dc, 128, cfg.dvb // 128, 128], F32, kind="ExternalInput").ap()
    if has_ffn:
        pg = nc.dram_tensor("pg", [128, dff], I32, kind="ExternalInput").ap()
        sg = nc.dram_tensor("sg", [128, dff], F32, kind="ExternalInput").ap()
        pu = nc.dram_tensor("pu", [128, dff], I32, kind="ExternalInput").ap()
        su = nc.dram_tensor("su", [128, dff], F32, kind="ExternalInput").ap()
        ndr = dff // 2 // 8
        pdt = nc.dram_tensor("pd", [ndr, cfg.d], I32, kind="ExternalInput").ap()
        sdt = nc.dram_tensor("sd", [ndr, cfg.d], F32, kind="ExternalInput").ap()
    idf = nc.dram_tensor("idf", [128, 128], F32, kind="ExternalInput").ap()
    ntab = max(4, len(build_tab(cfg, 0)))
    tabt = nc.dram_tensor("tab", [1, ntab], I32, kind="ExternalInput").ap()
    b0 = build_bias(cfg, 0)
    bias_in = {}
    if "a" in kinds:
        bias_in["bra"] = nc.dram_tensor("bra", [max(1, len(b0[0]))], F32, kind="ExternalInput").ap()
        bias_in["sca"] = nc.dram_tensor("sca", [max(1, len(b0[1]))], F32, kind="ExternalInput").ap()
    if "b" in kinds:
        bias_in["brb"] = nc.dram_tensor("brb", [max(1, len(b0[2]))], F32, kind="ExternalInput").ap()
        bias_in["scb"] = nc.dram_tensor("scb", [max(1, len(b0[3]))], F32, kind="ExternalInput").ap()

    if has_ffn:
        ngu_rows = (dff // 128) * 128
        nclr = dff // 128 // 8
        wg_in = nc.dram_tensor("wg_in", [ngu_rows, 256], BF16)
        wu_in = nc.dram_tensor("wu_in", [ngu_rows, 256], BF16)
        wd_in = nc.dram_tensor("wd_in", [dc * 128, nclr * 128], BF16)
        wg_full = nc.dram_tensor("wg_full", [8 * ngu_rows, 256], BF16)
        wu_full = nc.dram_tensor("wu_full", [8 * ngu_rows, 256], BF16)
        wd_full = nc.dram_tensor("wd_full", [8 * dc * 128, nclr * 128], BF16)
    kv = {}
    for kind in kinds:
        dk = cfg.dka if kind == "a" else cfg.dvb
        wmax = max(nkb * kblk for (kblk, nkb) in
                   [kblocks(cfg, kind, p, qg) for p in range(2) for qg in range(nqg)])
        kv[kind] = dict(
            wmax=wmax,
            k_in=[nc.dram_tensor(f"k{kind}_in{p}", [dk, tpc], BF16) for p in range(2)],
            v_in=[nc.dram_tensor(f"v{kind}_in{p}", [tpc, dk], BF16) for p in range(2)],
            k_out=nc.dram_tensor(f"k{kind}_out", [2, 2 * dk, tpc], BF16),
            v_out=nc.dram_tensor(f"v{kind}_out", [2, 2 * tpc, dk], BF16),
            kg=nc.dram_tensor(f"k{kind}_g", [dk, PAD + cfg.s], BF16),
            vg=nc.dram_tensor(f"v{kind}_g", [PAD + cfg.s, dk], BF16),
            kwin=nc.dram_tensor(f"k{kind}_w", [dk, wmax], BF16),
            vwin=nc.dram_tensor(f"v{kind}_w", [wmax, dk], BF16),
        )
    lnrow = nc.dram_tensor("lnrow", [96, 1024], F32)
    mtd = nc.dram_tensor("mtd", [48, 128, 128], F32)
    dbg_atb = nc.dram_tensor("dbg_atb", [16, 128, 512], BF16) if DBG.get("atb") else None
    dbg_t = (nc.dram_tensor("dbgt", [16, 128, 512], F32, kind="ExternalOutput")
             if (DBG.get("dump") or DBG.get("dump_ao") or DBG.get("dump_xt") or DBG.get("dump_xt2") or DBG.get("dump_wg")) else None)
    slot = {"ln": 0, "mt": 0}

    ctx = ExitStack()
    with ctx:
        def sbuf(name, shape, dt):
            return ctx.enter_context(nc.sbuf_tensor(name, shape, dt))

        XT = sbuf("XT", [128, dc * tpc], F32)
        XN = sbuf("XN", [128, dc * tpc], BF16)
        IB = sbuf("IB", [128, 16 * tpc], BF16)
        QT = IB
        AO = XN[:, 0:dc * 512]
        onesf = sbuf("onesf", [128, 1], F32)
        idt = sbuf("idt", [128, 128], F32)
        qnw_sb = sbuf("qnw_sb", [128, 16], F32)
        knw_sb = sbuf("knw_sb", [128, 16], F32)
        skcb = sbuf("skcb", [128, 256], F32)
        icol = sbuf("icol", [128, 8], I32)
        tab_sb = sbuf("tab_sb", [1, ntab], I32)
        aw = sbuf("aw", [128, max(4096, dc * 256)], F32)
        _w0 = aw[:, 0:1024].bitcast(BF16)
        _w1 = aw[:, 1024:2048].bitcast(BF16)
        wsl = [_w0, _w0]
        wsl2 = [_w1, _w1]
        wvsl = aw[:, 0:dc * 256].bitcast(BF16)
        kstw = max([cfg.kva * 256 if "a" in kinds else 0,
                    (cfg.dvb // 128) * 128 if "b" in kinds else 0])
        vstw = max([2 * cfg.dka if "a" in kinds else 0, 1 * cfg.dvb if "b" in kinds else 0])
        need_kv = 2 * (kstw + vstw) * 2
        asz = max(3584, (need_kv + 3) // 4)
        ast = sbuf("ast", [128, asz], F32)
        dq_i = ast[:, 0:512].bitcast(I32)
        dq_s = ast[:, 512:1024]
        dq_t = [ast[:, 1024 + 512 * i:1024 + 512 * i + 512].bitcast(I32) for i in range(2)]
        dq_b = ast[:, 2048:2304].bitcast(BF16)
        dq_sh = ast[:, 2304:2816].bitcast(I32)
        kst = [ast[:, (kstw // 2) * i:(kstw // 2) * (i + 1)].bitcast(BF16) for i in range(2)]
        vst = [ast[:, kstw + (vstw // 2) * i: kstw + (vstw // 2) * (i + 1)].bitcast(BF16) for i in range(2)]
        a_s = sbuf("a_s", [128, max(2 * cfg.d, 4096)], F32)
        stg = [a_s[:, cfg.d * i:cfg.d * (i + 1)] for i in range(2)]
        mxp = a_s[:, 0:2048]
        psp = a_s[:, 2048:4096]
        assert kstw // 2 + vstw // 2 <= 2048
        kst4 = kst + [a_s[:, 0:kstw // 2].bitcast(BF16),
                      a_s[:, 2048:2048 + kstw // 2].bitcast(BF16)]
        vst4 = vst + [a_s[:, kstw // 2:kstw // 2 + vstw // 2].bitcast(BF16),
                      a_s[:, 2048 + kstw // 2:2048 + kstw // 2 + vstw // 2].bitcast(BF16)]
        ssc = [sbuf(f"ssc{i}", [128, 512], F32) for i in range(4)]
        atb = [sbuf(f"atb{i}", [128, 512], BF16) for i in range(4)]
        sqs = [ssc[0], ssc[1]]
        kvstg = [atb[1], atb[2]]
        scrapb = atb[0]
        ab = sbuf("ab", [128, 2048], F32)
        bias2 = [ab[:, 512 * i:512 * i + 512] for i in range(4)]
        nmld = bias2
        nmb = ssc[3][:, 0:128]
        ldb = ssc[3][:, 128:256]
        mtt = ssc[3][:, 256:384]
        rowa = ab[0:1, 1536:2048]
        rowb = a_s[0:1, 0:512]
        rowc = a_s[0:1, 512:1024]
        bcm = [ab[:, 0:512], ab[:, 512:1024]]
        bcr = [ab[:, 1024:1536]] * 2
        gfb = ssc[2]
        gf2 = ssc[3]

        pp = [ctx.enter_context(nc.psum_tensor(f"pp{i}", [128, 1024], F32)) for i in range(4)]

        SM = {}
        for name in ["sx", "swA", "swB", "sw2A", "sw2B", "swv", "skvA", "skvB", "skv2",
                     "sbias", "srow", "smt", "scc", "sTV", "sRD", "sVS", "sSV", "sEV",
                     "sVG", "sGV", "sOUT", "sdq", "sST", "sdbg", "sfA", "sf2A",
                     "sccW", "sxio"]:
            SM[name] = Sem(nc, ctx, name)
        sx = SM["sx"]; swp = [SM["swA"], SM["swA"]]; sw2p = [SM["sw2A"], SM["sw2A"]]
        sfp = [SM["sfA"], SM["sfA"]]; sf2p = [SM["sf2A"], SM["sf2A"]]
        swv = SM["swv"]; skvp = [SM["skvA"], SM["skvB"]]; skv2 = SM["skv2"]
        sbias, srow, smt, scc = SM["sbias"], SM["srow"], SM["smt"], SM["scc"]
        sTV, sRD, sVS, sSV, sEV = SM["sTV"], SM["sRD"], SM["sVS"], SM["sSV"], SM["sEV"]
        sVG, sGV, sOUT, sdq, sST = SM["sVG"], SM["sGV"], SM["sOUT"], SM["sdq"], SM["sST"]
        sccW, sxio = SM["sccW"], SM["sxio"]
        ffn_ags = [[]]

        sdmp = SM["sdbg"]

        def dbg_dump(slot_i, src, bf16=False, w=512):
            if not (DBG.get("dump") or DBG.get("dump_ao") or DBG.get("dump_xt") or DBG.get("dump_xt2") or DBG.get("dump_wg")):
                return
            with nc.Block() as blk:
                @blk.sync
                def _(sy):
                    dst = dbg_t.ap()[slot_i]
                    if bf16:
                        dst = dst.rearrange("p q -> p q")[:, 0:(w + 1) // 2].bitcast(BF16)
                    else:
                        dst = dst[:, 0:w]
                    sdmp.dinc(sy.dma_start(out=dst, in_=src))
                    sdmp.wait(sy)

        rko = nc.sync.alloc_register("rko")
        rvo = nc.sync.alloc_register("rvo")
        rfill = nc.gpsimd.alloc_register("rfill")
        tab_pos = [0]
        bias_off = {"a": [0], "b": [0]}
        skc_off = {"a": [0], "b": [0]}

        def load_offsets(sy):
            i = tab_pos[0]
            sy.reg_load(rko, tab_sb[0:1, i:i + 1])
            sy.reg_load(rvo, tab_sb[0:1, i + 1:i + 2])
            tab_pos[0] += 2

        with nc.Block() as blk:
            @blk.sync
            def _(sy):
                sxio.dinc(sy.dma_start(out=idt[:, :], in_=idf))
                sxio.dinc(sy.dma_start(out=tab_sb[:, :], in_=tabt))
                if "a" in kinds:
                    sxio.dinc(sy.dma_start(out=qnw_sb[:, 0:cfg.dqa // 128],
                                         in_=qnw.rearrange("(c p) -> p c", p=128),
                                         allow_slow_non_contiguous=True))
                    sxio.dinc(sy.dma_start(out=knw_sb[:, 0:cfg.dka // 128],
                                         in_=knw.rearrange("(c p) -> p c", p=128),
                                         allow_slow_non_contiguous=True))

            @blk.vector
            def _(v):
                v.memset(onesf[:, :], 1.0)
                for j, val in enumerate([28, 22, 24, 18, -2147483648, 0x01C00000]):
                    v.memset(icol[:, j:j + 1], val)

            @blk.gpsimd
            def _(g):
                import struct as _st
                g.reg_mov(rfill, _st.unpack("<i", _st.pack("<f", NEG))[0])

        if has_ffn:
            dq_i2 = [ast[:, 0:512].bitcast(I32), ast[:, 512:1024].bitcast(I32)]
            dq_s2 = [ast[:, 1024:1536], ast[:, 1536:2048]]
            dq_t0 = ast[:, 2048:2560].bitcast(I32)
            dq_t1 = ast[:, 2560:3072].bitcast(I32)
            dq_b2 = [ast[:, 3072:3328].bitcast(BF16), ast[:, 3328:3584].bitcast(BF16)]

            items = []
            for src_p, src_s, nch, dst, dkind in [
                    (pg, sg, 1, wg_in, "gu"), (pu, su, 1, wu_in, "gu"),
                    (pdt, sdt, ndr // 128, wd_in, "d")]:
                nsl = src_p.shape[1] // 512
                for ch in range(nch):
                    for j in range(nsl):
                        items.append((src_p, src_s, ch, j, dst, dkind))
            kend = {}
            for m, (src_p, src_s, ch, j, dst, dkind) in enumerate(items):
                kend[id(dst)] = 2 * m + 2
            sx0, rd0, dq0, sv0 = sx.n, sRD.n, sdq.n, sVS.n

            def out_ap(dst, dkind, ch, j, hi):
                if dkind == "gu":
                    base = (4 * j) * 128 * 256 + hi * 128
                    return bass.AP(dst, base, [[256, 128], [128 * 256, 4], [1, 128]])
                ncl = 2 * ch + hi
                return bass.AP(dst, ncl * 128 + j * 4 * (128 * nclr * 128),
                               [[nclr * 128, 128], [128 * nclr * 128, 4], [1, 128]])

            with nc.Block() as blk:
                @blk.sync
                def _(sy):
                    for m, (src_p, src_s, ch, j, dst, dkind) in enumerate(items):
                        if m >= 2:
                            sy.wait_ge(sRD.h, rd0 + 2 * (m - 1))
                        sx.dinc(sy.dma_start(out=dq_i2[m % 2][:, :],
                                             in_=src_p[128 * ch:128 * ch + 128, 512 * j:512 * j + 512]))
                        sx.dinc(sy.dma_start(out=dq_s2[m % 2][:, :],
                                             in_=src_s[128 * ch:128 * ch + 128, 512 * j:512 * j + 512]))

                @blk.vector
                def _(v):
                    for m, (src_p, src_s, ch, j, dst, dkind) in enumerate(items):
                        v.wait_ge(sx.h, sx0 + 32 * (m + 1))
                        for hi in (0, 1):
                            k = 2 * m + hi
                            c1, c2 = (2, 3) if hi else (0, 1)
                            v.tensor_single_scalar(dq_t0[:, :], dq_i2[m % 2][:, :], icol[:, c1:c1 + 1], OP.logical_shift_left)
                            v.tensor_single_scalar(dq_t0[:, :], dq_t0[:, :], icol[:, 4:5], OP.bitwise_and)
                            v.tensor_single_scalar(dq_t1[:, :], dq_i2[m % 2][:, :], icol[:, c2:c2 + 1], OP.logical_shift_left)
                            v.tensor_single_scalar(dq_t1[:, :], dq_t1[:, :], icol[:, 5:6], OP.bitwise_and)
                            v.tensor_tensor(dq_t0[:, :], dq_t0[:, :], dq_t1[:, :], OP.bitwise_or)
                            if k >= 2:
                                v.wait_ge(sdq.h, dq0 + 16 * (k - 1))
                            sRD.inc(v.scalar_tensor_tensor(dq_b2[k % 2][:, :], dq_t0[:, :].bitcast(F32),
                                                           float(2.0 ** 126), dq_s2[m % 2][:, :],
                                                           OP.mult, OP.mult))

                @blk.gpsimd
                def _(g):
                    for m, (src_p, src_s, ch, j, dst, dkind) in enumerate(items):
                        for hi in (0, 1):
                            k = 2 * m + hi
                            g.wait_ge(sRD.h, rd0 + k + 1)
                            sdq.dinc(g.dma_start(out=out_ap(dst, dkind, ch, j, hi),
                                                 in_=dq_b2[k % 2][:, :].rearrange("p (t q) -> p t q", q=128)))
                    g.wait_ge(sdq.h, dq0 + 16 * 2 * len(items))
                    for ain, aout in [(wg_in, wg_full), (wu_in, wu_full), (wd_in, wd_full)]:
                        scc.inc(g.collective_compute("AllGather", OP.bypass,
                                                     replica_groups=[list(range(8))],
                                                     ins=[ain.ap().opt()], outs=[aout.ap().opt()]))
                    scc.wait(g)
            if DBG.get("dump_wfull"):
                wouts = []
                for nm, srct in [("wgo", wg_full), ("wuo", wu_full), ("wdo", wd_full)]:
                    o = nc.dram_tensor(nm, list(srct.shape), BF16, kind="ExternalOutput")
                    wouts.append((srct, o))
                with nc.Block() as blk:
                    @blk.sync
                    def _(sy):
                        for srct, o in wouts:
                            for r0 in range(0, srct.shape[0], 8192):
                                sdmp.dinc(sy.dma_start(out=o.ap()[r0:r0 + 8192, :],
                                                       in_=srct.ap()[r0:r0 + 8192, :]))
                        sdmp.wait(sy)
            if DBG.get("dump_wg"):
                for sl, srct in enumerate([wg_in, wu_in, wd_in]):
                    dbg_dump(sl, srct.ap()[128 * min(40, srct.shape[0] // 128 - 1):128 * min(40, srct.shape[0] // 128 - 1) + 128, 0:min(1024, srct.shape[1])], bf16=True, w=min(1024, srct.shape[1]))
                for sl, srct in enumerate([wg_full, wu_full, wd_full]):
                    dbg_dump(3 + sl, srct.ap()[128 * min(40, srct.shape[0] // 128 - 1):128 * min(40, srct.shape[0] // 128 - 1) + 128, 0:min(1024, srct.shape[1])], bf16=True, w=min(1024, srct.shape[1]))

        def bounce_rows(rows_dsts):
            sl = slot["ln"]
            slot["ln"] += 1
            assert sl < 96
            with nc.Block() as blk:
                @blk.sync
                def _(sy):
                    for i, (row, _) in enumerate(rows_dsts):
                        srow.dinc(sy.dma_start(out=bass.AP(lnrow, sl * 1024 + 512 * i, [[512, 1], [1, 512]]),
                                               in_=row[:, :]))
                    srow.wait(sy)
                    for i, (_, dst) in enumerate(rows_dsts):
                        srow.dinc(sy.dma_start(out=dst[:, :],
                                               in_=bass.AP(lnrow, sl * 1024 + 512 * i, [[0, 128], [1, 512]])))

        def ln_stage():
            for th in range(nth):
                t0c = 512 * th
                with nc.Block() as blk:
                    @blk.tensor
                    def _(t):
                        for c in range(dc):
                            t.matmul(pp[3][0:1, 0:512], onesf[:, :],
                                     XT[:, c * tpc + t0c: c * tpc + t0c + 512],
                                     start=(c == 0), stop=(c == dc - 1))
                base_sv, base_st = sSV.n, sST.n
                with nc.Block() as blk:
                    @blk.scalar
                    def _(s):
                        for c in range(dc):
                            if c >= 2:
                                s.wait_ge(sST.h, base_st + c - 1)
                            sSV.inc(s.activation(sqs[c % 2][:, :], XT[:, c * tpc + t0c: c * tpc + t0c + 512], AF.Square))

                    @blk.tensor
                    def _(t):
                        for c in range(dc):
                            t.wait_ge(sSV.h, base_sv + c + 1)
                            sST.inc(t.matmul(pp[3][0:1, 512:1024], onesf[:, :], sqs[c % 2][:, :],
                                             start=(c == 0), stop=(c == dc - 1)))
                with nc.Block() as blk:
                    @blk.vector
                    def _(v):
                        v.tensor_scalar(rowa[:, :], pp[3][0:1, 0:512], 1.0 / cfg.d, None, OP.mult)
                        v.tensor_tensor(rowc[:, :], rowa[:, :], rowa[:, :], OP.mult)
                        v.scalar_tensor_tensor(rowb[:, :], pp[3][0:1, 512:1024], 1.0 / cfg.d, rowc[:, :], OP.mult, OP.subtract)
                        v.tensor_scalar(rowb[:, :], rowb[:, :], 1e-5, None, OP.add)
                with nc.Block() as blk:
                    @blk.scalar
                    def _(s):
                        s.activation(rowb[:, :], rowb[:, :], AF.Sqrt)
                with nc.Block() as blk:
                    @blk.vector
                    def _(v):
                        v.reciprocal(rowb[:, :], rowb[:, :])
                bounce_rows([(rowa, bcm[th % 2]), (rowb, bcr[th % 2])])
                with nc.Block() as blk:
                    @blk.vector
                    def _(v):
                        srow.wait(v)
                        for c in range(dc):
                            cs = slice(c * tpc + t0c, c * tpc + t0c + 512)
                            v.tensor_tensor(ssc[0][:, :], XT[:, cs], bcm[th % 2][:, :], OP.subtract)
                            v.tensor_tensor(XN[:, cs], ssc[0][:, :], bcr[th % 2][:, :], OP.mult)

        def projT(srcw, njt, dst, scale, rms, t0=0, tw=None, dstw=None):
            tw = tpc if tw is None else tw
            dstw = tw if dstw is None else dstw
            nh = tw // 512
            w0, tv0, rd0, sv0, st0 = swp[0].n, sTV.n, sRD.n, sVS.n, sST.n
            bufs = [_w0, _w1]
            sq4 = [ssc[0], ssc[1], ssc[2], ssc[3]]
            nk = njt * nh

            with nc.Block() as blk:
                @blk.gpsimd
                def _(g):
                    for j in range(njt):
                        if j >= 2:
                            g.wait_ge(sTV.h, tv0 + j - 1)
                        swp[0].dinc(g.dma_start(out=bufs[j % 2][:, 0:dc * 128], in_=srcw[j]))

                @blk.tensor
                def _(t):
                    def ones_mm(j):
                        for th in range(nh):
                            k = j * nh + th
                            t.wait_ge(sVS.h, sv0 + k + 1)
                            sST.inc(t.matmul(pp[3][0:1, 512 * th:512 * th + 512], onesf[:, :],
                                             sq4[k % 4][:, :],
                                             start=(j == 0), stop=(j == njt - 1)))
                    for j in range(njt):
                        t.wait_ge(swp[0].h, w0 + 16 * (j + 1))
                        if j >= 2:
                            t.wait_ge(sRD.h, rd0 + j - 1)
                            if rms:
                                t.wait_ge(sVS.h, sv0 + (j - 1) * nh)
                        ins = None
                        for c in range(dc):
                            for th in range(nh):
                                ins = t.matmul(pp[j % 2][:, 512 * th:512 * th + 512],
                                               bufs[j % 2][:, c * 128:c * 128 + 128],
                                               XN[:, c * tpc + t0 + 512 * th: c * tpc + t0 + 512 * th + 512],
                                               start=(c == 0), stop=(c == dc - 1))
                        sTV.inc(ins)
                        if rms and j >= 1:
                            ones_mm(j - 1)
                    if rms:
                        ones_mm(njt - 1)

                @blk.vector
                def _(v):
                    for j in range(njt):
                        v.wait_ge(sTV.h, tv0 + j + 1)
                        if scale is not None:
                            ins = v.tensor_scalar(dst[:, j * dstw:j * dstw + tw], pp[j % 2][:, 0:tw],
                                                  float(scale), None, OP.mult)
                        else:
                            ins = v.tensor_copy(dst[:, j * dstw:j * dstw + tw], pp[j % 2][:, 0:tw])
                        sRD.inc(ins)

                if rms:
                    @blk.scalar
                    def _(s):
                        for j in range(njt):
                            s.wait_ge(sTV.h, tv0 + j + 1)
                            s.wait_ge(sRD.h, rd0 + j + 1)
                            for th in range(nh):
                                k = j * nh + th
                                if k >= 4:
                                    s.wait_ge(sST.h, st0 + k - 3)
                                sVS.inc(s.activation(sq4[k % 4][:, :],
                                                     pp[j % 2][:, 512 * th:512 * th + 512], AF.Square))

        def rms_apply(dst, njt, ndim, eps, extra, wcols, dstw=None, nh=None):
            dstw = tpc if dstw is None else dstw
            nh = nth if nh is None else nh
            for th in range(nh):
                with nc.Block() as blk:
                    @blk.vector
                    def _(v):
                        v.tensor_scalar(rowa[:, :], pp[3][0:1, 512 * th:512 * th + 512],
                                        1.0 / ndim, float(eps), OP.mult, OP.add)
                with nc.Block() as blk:
                    @blk.scalar
                    def _(s):
                        s.activation(rowa[:, :], rowa[:, :], AF.Sqrt)
                with nc.Block() as blk:
                    @blk.vector
                    def _(v):
                        v.reciprocal(rowa[:, :], rowa[:, :])
                        if extra != 1.0:
                            v.tensor_scalar(rowa[:, :], rowa[:, :], float(extra), None, OP.mult)
                bounce_rows([(rowa, bcm[th % 2])])
            with nc.Block() as blk:
                @blk.vector
                def _(v):
                    srow.wait(v)
                    for j in range(njt):
                        for th in range(nh):
                            cs = slice(j * dstw + 512 * th, j * dstw + 512 * th + 512)
                            if wcols is not None:
                                v.scalar_tensor_tensor(dst[:, cs], dst[:, cs], wcols[:, j:j + 1],
                                                       bcm[th % 2][:, :], OP.mult, OP.mult)
                            else:
                                v.tensor_tensor(dst[:, cs], dst[:, cs], bcm[th % 2][:, :], OP.mult)

        def dma_out_rows(dst_dram, src_sb, nkt):
            with nc.Block() as blk:
                @blk.sync
                def _(sy):
                    for j in range(nkt):
                        skv2.dinc(sy.dma_start(out=dst_dram.ap()[128 * j:128 * j + 128, :],
                                               in_=src_sb[:, j * tpc:(j + 1) * tpc]))

        def proj_v(srcw, ndvs, dst_dram):
            for dvs in range(ndvs):
                tv0, rd0, kv0 = sTV.n, sRD.n, skv2.n
                with nc.Block() as blk:
                    @blk.gpsimd
                    def _(g):
                        swv.dinc(g.dma_start(out=wvsl[:, 0:dc * 512], in_=srcw[dvs]))

                    @blk.tensor
                    def _(t):
                        swv.wait(t)
                        for tt in range(ntt):
                            if tt >= 2:
                                t.wait_ge(sRD.h, rd0 + tt - 1)
                            ins = None
                            for c in range(dc):
                                ins = t.matmul(pp[tt % 2][:, 0:512],
                                               XN[:, c * tpc + 128 * tt: c * tpc + 128 * tt + 128],
                                               wvsl[:, c * 512:c * 512 + 512],
                                               start=(c == 0), stop=(c == dc - 1))
                            sTV.inc(ins)

                    @blk.vector
                    def _(v):
                        for tt in range(ntt):
                            v.wait_ge(sTV.h, tv0 + tt + 1)
                            if tt >= 2:
                                v.wait_ge(skv2.h, kv0 + 16 * (tt - 1))
                            sRD.inc(v.tensor_copy(kvstg[tt % 2][:, 0:512], pp[tt % 2][:, 0:512]))

                    @blk.sync
                    def _(sy):
                        for tt in range(ntt):
                            sy.wait_ge(sRD.h, rd0 + tt + 1)
                            skv2.dinc(sy.dma_start(
                                out=dst_dram.ap()[128 * tt:128 * tt + 128, 512 * dvs:512 * dvs + 512],
                                in_=kvstg[tt % 2][:, 0:512]))

        def oproj(srcw, ndvc, qg):
            qgl = 512 * qg
            w0, tv0, rd0 = swp[0].n, sTV.n, sRD.n
            bufs = [_w0, _w1]
            with nc.Block() as blk:
                @blk.gpsimd
                def _(g):
                    for j in range(dc):
                        if j >= 2:
                            g.wait_ge(sTV.h, tv0 + j - 1)
                        swp[0].dinc(g.dma_start(out=bufs[j % 2][:, 0:ndvc * 128], in_=srcw[j]))

                @blk.tensor
                def _(t):
                    for j in range(dc):
                        t.wait_ge(swp[0].h, w0 + 16 * (j + 1))
                        if j >= 2:
                            t.wait_ge(sRD.h, rd0 + j - 1)
                        ins = None
                        for c in range(ndvc):
                            ins = t.matmul(pp[j % 2][:, 0:512], bufs[j % 2][:, c * 128:c * 128 + 128],
                                           AO[:, c * 512:c * 512 + 512],
                                           start=(c == 0), stop=(c == ndvc - 1))
                        sTV.inc(ins)

                @blk.vector
                def _(v):
                    for j in range(dc):
                        v.wait_ge(sTV.h, tv0 + j + 1)
                        cs = slice(j * tpc + qgl, j * tpc + qgl + 512)
                        sRD.inc(v.tensor_tensor(XT[:, cs], XT[:, cs], pp[j % 2][:, 0:512], OP.add))

        def attention(kind, pidx, ow, ndvc, qsrc):
            isa = kind == "a"
            H = cfg.ha if isa else cfg.hb
            DH = cfg.dha if isa else cfg.dhb
            KVH = cfg.kva if isa else cfg.hb
            rep = H // KVH
            dk = KVH * DH
            nkch = dk // 128
            kk = kv[kind]
            wmax = kk["wmax"]
            bra_t = bias_in["bra" if isa else "brb"]
            sca_t = bias_in["sca" if isa else "scb"]
            HG = cfg.hg
            if isa:
                projT(qsrc, cfg.dqa // 128, QT, None, True)
                rms_apply(QT, cfg.dqa // 128, cfg.dqa, 1e-6, 1.0 / math.sqrt(cfg.dha), qnw_sb)
            else:
                projT(qsrc, cfg.dvb // 128, QT, 1.0 / math.sqrt(cfg.dhb), False)
            for qg in range(nqg):
                kblk, nkb = kblocks(cfg, kind, pidx, qg)
                ncpb = kblk // 128
                W = nkb * kblk
                qgl = 512 * qg
                relk = [512 - (nkb - i) * kblk for i in range(nkb)]
                boff0 = bias_off[kind][0]
                soff0 = skc_off[kind][0]
                with nc.Block() as blk:
                    @blk.sync
                    def _(sy):
                        sEV.wait(sy)
                        load_offsets(sy)
                        skv2.dinc(sy.dma_start(
                            out=bass.AP(kk["kwin"], 0, [[wmax, dk], [1, W]]),
                            in_=bass.AP(kk["kg"], rko, [[PAD + cfg.s, dk], [1, W]])))
                        skv2.dinc(sy.dma_start(
                            out=bass.AP(kk["vwin"], 0, [[dk, W], [1, dk]]),
                            in_=bass.AP(kk["vg"], rvo, [[dk, W], [1, dk]])))
                dbg_here = (DBG.get("dump") and kind == DBG.get("kind")
                            and pidx == DBG.get("pidx") and qg == DBG.get("qg"))
                if dbg_here:
                    dbg_dump(0, QT[:, 0:512], bf16=True)
                    dbg_dump(1, QT[:, 512:1024], bf16=True)
                with nc.Block() as blk:
                    @blk.vector
                    def _(v):
                        sEV.wait(v)
                        v.memset(mxp[:, 0:H * 4 * nkb], NEG)

                    @blk.sync
                    def _(sy):
                        skv2.wait(sy)

                tiles = []
                for kbi in range(nkb):
                    for h in range(H):
                        for qt in range(4):
                            r0 = relk[kbi] - 128 * qt
                            up = (r0 + kblk - 1) > 0
                            wn = isa and (r0 < -cfg.win + 127)
                            tiles.append((kbi, h, qt, up or wn))
                tpkb = H * 4
                tv0, rd0, sv0 = sTV.n, sRD.n, sSV.n
                gv0, vg0 = sGV.n, sVG.n
                kv0 = [skvp[0].n, skvp[1].n]
                bias0 = sbias.n
                epre = []
                ec = 0
                for (_, _, _, e) in tiles:
                    epre.append(ec)
                    ec += 1 if e else 0

                with nc.Block() as blk:
                    @blk.sync
                    def _(sy):
                        for kbi in range(nkb):
                            if kbi >= 2:
                                sy.wait_ge(sTV.h, tv0 + (kbi - 1) * tpkb)
                            if isa:
                                for rr in (0, 1):
                                    skvp[kbi % 2].dinc(sy.dma_start(
                                        out=kst[kbi % 2][64 * rr:64 * rr + 64, 0:KVH * kblk]
                                        .rearrange("p (c k) -> p c k", k=kblk),
                                        in_=bass.AP(kk["kwin"], kbi * kblk,
                                                    [[wmax, 64], [64 * wmax, KVH], [1, kblk]])))
                            else:
                                skvp[kbi % 2].dinc(sy.dma_start(
                                    out=kst[kbi % 2][:, 0:nkch * kblk].rearrange("p (c k) -> p c k", k=kblk),
                                    in_=bass.AP(kk["kwin"], kbi * kblk, [[wmax, 128], [128 * wmax, nkch], [1, kblk]])))
                            for h in range(H):
                                if h >= 4:
                                    sy.wait_ge(sRD.h, rd0 + kbi * tpkb + (h - 4) * 4 + 4)
                                elif kbi >= 1:
                                    sy.wait_ge(sRD.h, rd0 + (kbi - 1) * tpkb + (H - 4 + h) * 4 + 4)
                                sbias.dinc(sy.dma_start(
                                    out=bias2[h % 4][:, 0:kblk],
                                    in_=bass.AP(bra_t.tensor, boff0 + kbi * H * kblk + h * kblk,
                                                [[0, 128], [1, kblk]])))

                    @blk.tensor
                    def _(t):
                        for j, (kbi, h, qt, edge) in enumerate(tiles):
                            sl = j % 4
                            bank = pp[2 + sl // 2][:, 512 * (sl % 2):512 * (sl % 2) + 512]
                            kd = 2 if isa else 1
                            t.wait_ge(skvp[kbi % 2].h, kv0[kbi % 2] + 16 * kd * (kbi // 2 + 1))
                            if j >= 4:
                                t.wait_ge(sRD.h, rd0 + j - 3)
                            kvh = h // rep
                            qrow = (h * DH) // 128
                            qoff = (h * DH) % 128
                            cch = kvh if isa else (kvh * DH) // 128
                            coff = qoff if isa else (kvh * DH) % 128
                            sTV.inc(t.matmul(
                                bank[:, 0:kblk],
                                QT[qoff:qoff + DH, qrow * tpc + qgl + 128 * qt: qrow * tpc + qgl + 128 * qt + 128],
                                kst[kbi % 2][coff:coff + DH, cch * kblk: cch * kblk + kblk],
                                start=True, stop=True))

                    @blk.vector
                    def _(v):
                        for j, (kbi, h, qt, edge) in enumerate(tiles):
                            sl = j % 4
                            bank = pp[2 + sl // 2][:, 512 * (sl % 2):512 * (sl % 2) + 512]
                            v.wait_ge(sTV.h, tv0 + j + 1)
                            v.wait_ge(sbias.h, bias0 + 16 * (kbi * H + h + 1))
                            if j >= 4:
                                v.wait_ge(sSV.h, sv0 + j - 3)
                            colidx = (h * 4 + qt) * nkb + kbi
                            if isa:
                                ins = v.tensor_tensor(ssc[sl][:, 0:kblk], bank[:, 0:kblk], bias2[h % 4][:, 0:kblk], OP.add)
                            else:
                                v.tensor_scalar(ssc[sl][:, 0:kblk], bank[:, 0:kblk], 1024.0, -1024.0, OP.min, OP.max)
                                ins = v.tensor_tensor(ssc[sl][:, 0:kblk], ssc[sl][:, 0:kblk], bias2[h % 4][:, 0:kblk], OP.add)
                            if edge:
                                sVG.inc(ins)
                                v.wait_ge(sGV.h, gv0 + epre[j] + 1)
                            sRD.inc(v.reduce_max(mxp[:, colidx:colidx + 1], ssc[sl][:, 0:kblk], AX.X, negate=True))

                    @blk.gpsimd
                    def _(g):
                        for j, (kbi, h, qt, edge) in enumerate(tiles):
                            if not edge:
                                continue
                            sl = j % 4
                            r0 = relk[kbi] - 128 * qt
                            g.wait_ge(sVG.h, vg0 + epre[j] + 1)
                            ins = None
                            if (r0 + kblk - 1) > 0:
                                ins = g.affine_select(out=ssc[sl][:, 0:kblk], in_=ssc[sl][:, 0:kblk],
                                                      pattern=[[-1, kblk]], compare_op=OP.is_ge,
                                                      fill=rfill, base=-r0, channel_multiplier=1)
                            if isa and (r0 < -cfg.win + 127):
                                ins = g.affine_select(out=ssc[sl][:, 0:kblk], in_=ssc[sl][:, 0:kblk],
                                                      pattern=[[1, kblk]], compare_op=OP.is_ge,
                                                      fill=rfill, base=r0 + cfg.win, channel_multiplier=-1)
                            sGV.inc(ins)

                    @blk.scalar
                    def _(s):
                        for j, (kbi, h, qt, edge) in enumerate(tiles):
                            sl = j % 4
                            colidx = (h * 4 + qt) * nkb + kbi
                            s.wait_ge(sRD.h, rd0 + j + 1)
                            sSV.inc(s.activation(scrapb[:, 0:kblk], ssc[sl][:, 0:kblk], AF.Exp,
                                                 bias=mxp[:, colidx:colidx + 1],
                                                 accum_out=psp[:, colidx:colidx + 1]))

                if dbg_here:
                    dbg_dump(2, mxp[:, 0:512])
                    dbg_dump(3, psp[:, 0:512])
                    dbg_dump(4, kst[(nkb - 1) % 2][:, 0:512], bf16=True)
                with nc.Block() as blk:
                    @blk.vector
                    def _(v):
                        for hq in range(H * 4):
                            v.tensor_reduce(nmb[:, hq:hq + 1], mxp[:, hq * nkb:(hq + 1) * nkb], AX.X, OP.min)
                        for hq in range(H * 4):
                            v.tensor_scalar(mxp[:, hq * nkb:(hq + 1) * nkb], mxp[:, hq * nkb:(hq + 1) * nkb],
                                            nmb[:, hq:hq + 1], None, OP.subtract)
                with nc.Block() as blk:
                    @blk.scalar
                    def _(s):
                        s.activation(mxp[:, 0:H * 4 * nkb], mxp[:, 0:H * 4 * nkb], AF.Exp, scale=-1.0)
                with nc.Block() as blk:
                    @blk.vector
                    def _(v):
                        v.tensor_tensor(psp[:, 0:H * 4 * nkb], psp[:, 0:H * 4 * nkb], mxp[:, 0:H * 4 * nkb], OP.mult)
                        for hq in range(H * 4):
                            ins = v.reduce_sum(ldb[:, hq:hq + 1], psp[:, hq * nkb:(hq + 1) * nkb], AX.X)
                        sRD.inc(ins)
                with nc.Block() as blk:
                    @blk.scalar
                    def _(s):
                        s.activation(ldb[:, 0:H * 4], ldb[:, 0:H * 4], AF.Ln)
                with nc.Block() as blk:
                    @blk.vector
                    def _(v):
                        v.tensor_tensor(nmb[:, 0:H * 4], nmb[:, 0:H * 4], ldb[:, 0:H * 4], OP.subtract)
                        if H * 4 < 128:
                            v.memset(nmb[:, H * 4:128], 0.0)
                with nc.Block() as blk:
                    @blk.tensor
                    def _(t):
                        sTV.inc(t.transpose(pp[3][:, 512:640], nmb[:, :], idt[:, :]))

                    @blk.vector
                    def _(v):
                        sTV.wait(v)
                        v.tensor_copy(mtt[:, :], pp[3][:, 512:640])
                msl = slot["mt"]
                slot["mt"] += 1
                assert msl < 48
                with nc.Block() as blk:
                    @blk.sync
                    def _(sy):
                        smt.dinc(sy.dma_start(out=mtd.ap()[msl], in_=mtt[:, :]))
                        smt.wait(sy)
                if dbg_here:
                    dbg_dump(5, ssc[3][:, 0:512])

                nhg = (H + HG - 1) // HG
                rdF = sRD.n
                evq, kvq = sEV.n, [skvp[0].n, skvp[1].n]
                for hgi in range(nhg):
                    heads = list(range(hgi * HG, min(H, hgi * HG + HG)))
                    nh = len(heads)
                    with nc.Block() as blk:
                        @blk.sync
                        def _(sy):
                            for i, h in enumerate(heads):
                                smt.dinc(sy.dma_start(
                                    out=nmld[i][:, :],
                                    in_=bass.AP(mtd, msl * 16384 + (h * 4) * 128, [[0, 128], [1, 512]])))
                    tiles2 = []
                    for kbi in range(nkb):
                        for kc in range(ncpb):
                            for i in range(nh):
                                tiles2.append((kbi, kc, i))
                    n2 = len(tiles2)
                    first_of = {}
                    last_of = {}
                    for j, (kbi, kc, i) in enumerate(tiles2):
                        if i not in first_of:
                            first_of[i] = j
                        last_of[i] = j
                    tpkb2 = ncpb * nh
                    tv0, rd0, vs0, sv0, ev0 = sTV.n, sRD.n, sVS.n, sSV.n, sEV.n
                    gv0, vg0 = sGV.n, sVG.n
                    kv0 = [skvp[0].n, skvp[1].n]
                    smt0 = smt.n
                    e2 = []
                    nv_at = []
                    ne_at = []
                    nv = ne = 0
                    for (kbi, kc, i) in tiles2:
                        r0 = relk[kbi] + 128 * kc
                        e = ((r0 + 127) > 0) or (isa and (r0 < -cfg.win + 511))
                        if e:
                            ne += 1
                        else:
                            nv += 1
                        e2.append((e, ne - 1 if e else 0))
                        nv_at.append(nv)
                        ne_at.append(ne)
                    LAG = 3

                    dbg_on = (DBG.get("atb") and isa and pidx == 0 and qg == 0 and hgi == 0)
                    sdbg = SM["sdbg"]
                    with nc.Block() as blk:
                        @blk.sync
                        def _(sy):
                            if dbg_on:
                                sy.wait_ge(smt.h, smt0)
                                sdbg.dinc(sy.dma_start(
                                    out=dbg_atb.ap()[10].bitcast(mybir.dt.float32)[:, 0:256],
                                    in_=nmld[0][:, 0:256]))
                            for kbi in range(nkb):
                                if dbg_on and kbi == 2:
                                    sy.wait_ge(sVS.h, vs0 + nv_at[4])
                                    sdbg.dinc(sy.dma_start(
                                        out=dbg_atb.ap()[12].bitcast(mybir.dt.float32)[:, 0:256],
                                        in_=ssc[0][:, 0:256]))
                                if dbg_on and kbi >= 2:
                                    for m in range((kbi - 2) * ncpb, (kbi - 1) * ncpb):
                                        sy.wait_ge(sSV.h, sv0 + 4 * m + 1)
                                        sdbg.dinc(sy.dma_start(out=dbg_atb.ap()[m],
                                                               in_=atb[0][:, :]))
                                gg = hgi * nkb + kbi
                                if gg == 2:
                                    sy.wait_ge(sRD.h, rdF)
                                if gg >= 4:
                                    sy.wait_ge(sEV.h, evq + (gg - 3) * tpkb2)
                                if isa:
                                    for rr in (0, 1):
                                        skvp[gg % 2].dinc(sy.dma_start(
                                            out=kst4[gg % 4][64 * rr:64 * rr + 64, 0:KVH * kblk]
                                            .rearrange("p (c k) -> p c k", k=kblk),
                                            in_=bass.AP(kk["kwin"], kbi * kblk,
                                                        [[wmax, 64], [64 * wmax, KVH], [1, kblk]])))
                                else:
                                    skvp[gg % 2].dinc(sy.dma_start(
                                        out=kst4[gg % 4][:, 0:nkch * kblk].rearrange("p (c k) -> p c k", k=kblk),
                                        in_=bass.AP(kk["kwin"], kbi * kblk, [[wmax, 128], [128 * wmax, nkch], [1, kblk]])))
                                skvp[gg % 2].dinc(sy.dma_start(
                                    out=vst4[gg % 4][:, 0:ncpb * dk].rearrange("p (c d) -> p c d", d=dk),
                                    in_=bass.AP(kk["vwin"], kbi * kblk * dk, [[dk, 128], [128 * dk, ncpb], [1, dk]])))
                                skvp[gg % 2].dinc(sy.dma_start(
                                    out=skcb[:, 64 * (gg % 4):64 * (gg % 4) + H * ncpb],
                                    in_=bass.AP(sca_t.tensor, soff0 + kbi * H * ncpb * 128,
                                                [[H * ncpb, 128], [1, H * ncpb]])))
                            if dbg_on:
                                for m in range((nkb - 2) * ncpb, nkb * ncpb):
                                    sy.wait_ge(sSV.h, sv0 + 4 * m + 1)
                                    sdbg.dinc(sy.dma_start(out=dbg_atb.ap()[m],
                                                           in_=atb[0][:, :]))

                        @blk.tensor
                        def _(t):
                            def av(jj):
                                kbi_, kc_, i_ = tiles2[jj]
                                h_ = heads[i_]
                                kvh_ = h_ // rep
                                avb = pp[i_ // 2][:, 512 * (i_ % 2):512 * (i_ % 2) + 512]
                                t.wait_ge(sSV.h, sv0 + jj + 1)
                                sEV.inc(t.matmul(
                                    avb[0:DH, :],
                                    vst4[(hgi * nkb + kbi_) % 4][:, kc_ * dk + kvh_ * DH: kc_ * dk + kvh_ * DH + DH],
                                    atb[jj % 4][:, :],
                                    start=(jj == first_of[i_]), stop=(jj == last_of[i_])))
                            for j, (kbi, kc, i) in enumerate(tiles2):
                                h = heads[i]
                                kvh = h // rep
                                qrow = (h * DH) // 128
                                qoff = (h * DH) % 128
                                cch = kvh if isa else (kvh * DH) // 128
                                coff = qoff if isa else (kvh * DH) % 128
                                sl = j % 4
                                bank = pp[2 + sl // 2][:, 512 * (sl % 2):512 * (sl % 2) + 512]
                                kd2 = (2 if isa else 1) + 2
                                gg = hgi * nkb + kbi
                                t.wait_ge(skvp[gg % 2].h, kvq[gg % 2] + 16 * kd2 * (gg // 2 + 1))
                                if j >= 4:
                                    jj4 = j - 4
                                    if e2[jj4][0]:
                                        t.wait_ge(sVG.h, vg0 + ne_at[jj4])
                                    else:
                                        t.wait_ge(sVS.h, vs0 + nv_at[jj4])
                                sTV.inc(t.matmul(
                                    bank,
                                    kst4[gg % 4][coff:coff + DH, cch * kblk + 128 * kc: cch * kblk + 128 * kc + 128],
                                    QT[qoff:qoff + DH, qrow * tpc + qgl: qrow * tpc + qgl + 512],
                                    start=True, stop=True))
                                if j >= LAG:
                                    av(j - LAG)
                            for jj in range(max(0, n2 - LAG), n2):
                                av(jj)

                        @blk.vector
                        def _(v):
                            for j, (kbi, kc, i) in enumerate(tiles2):
                                h = heads[i]
                                sl = j % 4
                                bank = pp[2 + sl // 2][:, 512 * (sl % 2):512 * (sl % 2) + 512]
                                gg = hgi * nkb + kbi
                                v.wait_ge(sTV.h, tv0 + j + 1)
                                v.wait_ge(skvp[gg % 2].h, kvq[gg % 2] + 16 * ((2 if isa else 1) + 2) * (gg // 2 + 1))
                                v.wait_ge(smt.h, smt0)
                                if j >= 4:
                                    v.wait_ge(sSV.h, sv0 + j - 3)
                                kcolap = skcb[:, 64 * (gg % 4) + h * ncpb + kc:
                                              64 * (gg % 4) + h * ncpb + kc + 1]
                                if isa:
                                    ins = v.scalar_tensor_tensor(ssc[sl][:, :], bank, kcolap,
                                                                 nmld[i][:, :], OP.add, OP.add)
                                else:
                                    v.tensor_scalar(ssc[sl][:, :], bank, 1024.0, -1024.0, OP.min, OP.max)
                                    ins = v.scalar_tensor_tensor(ssc[sl][:, :], ssc[sl][:, :], kcolap,
                                                                 nmld[i][:, :], OP.add, OP.add)
                                if e2[j][0]:
                                    sVG.inc(ins)
                                else:
                                    sVS.inc(ins)

                        @blk.gpsimd
                        def _(g):
                            for j, (kbi, kc, i) in enumerate(tiles2):
                                if not e2[j][0]:
                                    continue
                                sl = j % 4
                                r0 = relk[kbi] + 128 * kc
                                g.wait_ge(sVG.h, vg0 + ne_at[j])
                                ins = None
                                if (r0 + 127) > 0:
                                    ins = g.affine_select(out=ssc[sl][:, :], in_=ssc[sl][:, :],
                                                          pattern=[[1, 512]], compare_op=OP.is_ge,
                                                          fill=rfill, base=-r0, channel_multiplier=-1)
                                if isa and (r0 < -cfg.win + 511):
                                    ins = g.affine_select(out=ssc[sl][:, :], in_=ssc[sl][:, :],
                                                          pattern=[[-1, 512]], compare_op=OP.is_ge,
                                                          fill=rfill, base=r0 + cfg.win, channel_multiplier=1)
                                sGV.inc(ins)

                        @blk.scalar
                        def _(s):
                            for j, (kbi, kc, i) in enumerate(tiles2):
                                sl = j % 4
                                if e2[j][0]:
                                    s.wait_ge(sGV.h, gv0 + ne_at[j])
                                else:
                                    s.wait_ge(sVS.h, vs0 + nv_at[j])
                                if j >= 4:
                                    s.wait_ge(sEV.h, ev0 + j - 3)
                                if dbg_on and sl == 0 and j >= 4:
                                    s.wait_ge(sdbg.h, 16 * (j // 4 + 2))
                                sSV.inc(s.activation(atb[sl][:, :], ssc[sl][:, :], AF.Exp))

                    with nc.Block() as blk:
                        @blk.vector
                        def _(v):
                            for i, h in enumerate(heads):
                                avb = pp[i // 2][:, 512 * (i % 2):512 * (i % 2) + 512]
                                drow = (h * DH) // 128
                                doff = (h * DH) % 128
                                v.tensor_copy(AO[doff:doff + DH, drow * 512:drow * 512 + 512], avb[0:DH, :])
                    if dbg_here and hgi == 0:
                        dbg_dump(6, nmld[0][:, :])
                        dbg_dump(7, vst[(nkb - 1) % 2][:, 0:512], bf16=True)
                        dbg_dump(8, AO[0:128, 0:512], bf16=True)
                        dbg_dump(9, atb[(len(tiles2) - 1) % 4][:, :], bf16=True)
                if DBG.get("dump_ao") and kind == DBG.get("kind") and pidx == DBG.get("pidx") and qg == DBG.get("qg"):
                    for c in range(min(16, ndvc)):
                        dbg_dump(c, AO[0:128, c * 512:(c + 1) * 512], bf16=True)
                bias_off[kind][0] += nkb * H * kblk
                skc_off[kind][0] += nkb * H * ncpb * 128
                oproj(ow, ndvc, qg)
            if DBG.get("dump_xt") and kind == DBG.get("kind") and pidx == DBG.get("pidx"):
                for c in range(dc):
                    dbg_dump(c, XT[:, c * tpc + DBG.get("xt_t0", 0):c * tpc + DBG.get("xt_t0", 0) + 512])

        def ffn():
            nblocks = dff // 2048
            ntt_g = nblocks * 16
            gbufs = [aw[:, 0:1024].bitcast(BF16), aw[:, 2048:3072].bitcast(BF16)]
            ubufs = [aw[:, 1024:2048].bitcast(BF16), aw[:, 3072:4096].bitcast(BF16)]
            dbufs = [ast[:, 0:1024].bitcast(BF16), ast[:, 1024:2048].bitcast(BF16)]
            fg0, fd0 = sfp[0].n, sf2p[0].n
            tg0, td0 = sTV.n, sST.n
            vg0, vd0 = sRD.n, sEV.n
            ga0, gb0 = sVS.n, sSV.n
            rpb = max(1, 16 // nclr)
            cpr = min(nclr, 16)

            with nc.Block() as blk:
                @blk.sync
                def _(sy):
                    for nt in range(ntt_g):
                        if nt >= 2:
                            sy.wait_ge(sTV.h, tg0 + 2 * (nt - 1))
                        src = bass.AP(wg_full, nt * 128 * 256, [[256, 128], [ngu_rows * 256, 8], [1, 256]])
                        sfp[0].dinc(sy.dma_start(out=gbufs[nt % 2][:, 0:2048].rearrange("p (r q) -> p r q", q=256), in_=src))
                        src2 = bass.AP(wu_full, nt * 128 * 256, [[256, 128], [ngu_rows * 256, 8], [1, 256]])
                        sfp[0].dinc(sy.dma_start(out=ubufs[nt % 2][:, 0:2048].rearrange("p (r q) -> p r q", q=256), in_=src2))

                @blk.gpsimd
                def _(g):
                    for nb in range(nblocks):
                        for dt in range(dc):
                            gdt = nb * dc + dt
                            if gdt >= 2:
                                g.wait_ge(sST.h, td0 + gdt - 1)
                            base = nb * (dc * 128 * nclr * 128) + dt * 128 * (nclr * 128)
                            src = bass.AP(wd_full, base,
                                          [[nclr * 128, 128], [dc * 128 * nclr * 128, rpb],
                                           [128, cpr], [1, 128]])
                            sf2p[0].dinc(g.dma_start(
                                out=dbufs[gdt % 2][:, 0:16 * 128].rearrange("p (r c q) -> p r c q", r=rpb, q=128),
                                in_=src))

                @blk.tensor
                def _(t):
                    for nt in range(ntt_g):
                        nb, ntl = nt // 16, nt % 16
                        ppg = pp[0 + 2 * (nt % 2)]
                        ppu = pp[1 + 2 * (nt % 2)]
                        t.wait_ge(sfp[0].h, fg0 + 32 * (nt + 1))
                        if nt >= 2:
                            t.wait_ge(sRD.h, vg0 + nt - 1)
                        if ntl == 1 and nb > 0:
                            t.wait_ge(sEV.h, vd0 + nb * dc)
                        gbuf, ubuf = gbufs[nt % 2], ubufs[nt % 2]
                        ins = None
                        for c in range(dc):
                            for th in range(nth):
                                ins = t.matmul(ppg[:, 512 * th:512 * th + 512], gbuf[:, c * 128:c * 128 + 128],
                                               XN[:, c * tpc + 512 * th: c * tpc + 512 * th + 512],
                                               start=(c == 0), stop=(c == dc - 1))
                        sTV.inc(ins)
                        for c in range(dc):
                            for th in range(nth):
                                ins = t.matmul(ppu[:, 512 * th:512 * th + 512], ubuf[:, c * 128:c * 128 + 128],
                                               XN[:, c * tpc + 512 * th: c * tpc + 512 * th + 512],
                                               start=(c == 0), stop=(c == dc - 1))
                        sTV.inc(ins)
                        if ntl == 15:
                            for dt in range(dc):
                                gdt = nb * dc + dt
                                ppd = pp[2 + gdt % 2]
                                t.wait_ge(sf2p[0].h, fd0 + 16 * (gdt + 1))
                                if dt == 0:
                                    t.wait_ge(sRD.h, vg0 + nt + 1)
                                if gdt >= 2:
                                    t.wait_ge(sEV.h, vd0 + gdt - 1)
                                dbuf = dbufs[gdt % 2]
                                for ncl in range(16):
                                    for th in range(nth):
                                        ins = t.matmul(ppd[:, 512 * th:512 * th + 512], dbuf[:, ncl * 128:ncl * 128 + 128],
                                                       IB[:, ncl * tpc + 512 * th: ncl * tpc + 512 * th + 512],
                                                       start=(ncl == 0), stop=(ncl == 15))
                                sST.inc(ins)

                @blk.vector
                def _(v):
                    for nt in range(ntt_g):
                        nb, ntl = nt // 16, nt % 16
                        ppg = pp[0 + 2 * (nt % 2)]
                        ppu = pp[1 + 2 * (nt % 2)]
                        v.wait_ge(sTV.h, tg0 + 2 * nt + 2)
                        for th in range(nth):
                            k = nt * nth + th
                            g_ps = ppg[:, 512 * th:512 * th + 512]
                            u_ps = ppu[:, 512 * th:512 * th + 512]
                            v.tensor_copy(gfb[:, :], g_ps)
                            v.tensor_tensor(gf2[:, :], gfb[:, :], gfb[:, :], OP.mult)
                            v.tensor_scalar(gf2[:, :], gf2[:, :], 0.044715, 1.0, OP.mult, OP.add)
                            sVS.inc(v.tensor_tensor(gf2[:, :], gf2[:, :], gfb[:, :], OP.mult))
                            v.wait_ge(sSV.h, gb0 + k + 1)
                            v.tensor_tensor(gf2[:, :], gf2[:, :], gfb[:, :], OP.mult)
                            ins = v.tensor_tensor(IB[:, ntl * tpc + 512 * th: ntl * tpc + 512 * th + 512],
                                                  gf2[:, :], u_ps, OP.mult)
                        sRD.inc(ins)
                        if ntl == 15:
                            for dt in range(dc):
                                gdt = nb * dc + dt
                                ppd = pp[2 + gdt % 2]
                                v.wait_ge(sST.h, td0 + gdt + 1)
                                sEV.inc(v.tensor_tensor(XT[:, dt * tpc:dt * tpc + tpc],
                                                        XT[:, dt * tpc:dt * tpc + tpc],
                                                        ppd[:, 0:tpc], OP.add))

                @blk.scalar
                def _(s):
                    for k in range(ntt_g * nth):
                        s.wait_ge(sVS.h, ga0 + k + 1)
                        sSV.inc(s.activation(gf2[:, :], gf2[:, :], AF.Sigmoid, scale=1.5957691216057308))

        def xio_in(pidx):
            tvX = sTV.n
            for tt in range(ntt):
                with nc.Block() as blk:
                    @blk.sync
                    def _(sy):
                        if tt >= 2:
                            sy.wait_ge(sTV.h, tvX + (tt - 1) * dc)
                        sxio.dinc(sy.dma_start(out=stg[tt % 2][:, :], in_=xin[pidx, 128 * tt:128 * tt + 128, :]))
                tv0, rd0 = sTV.n, sRD.n
                with nc.Block() as blk:
                    @blk.tensor
                    def _(t):
                        sxio.wait(t)
                        for c in range(dc):
                            if c >= 2:
                                t.wait_ge(sRD.h, rd0 + c - 1)
                            sTV.inc(t.transpose(pp[2 + c % 2][:, 512:640], stg[tt % 2][:, 128 * c:128 * c + 128], idt[:, :]))

                    @blk.vector
                    def _(v):
                        for c in range(dc):
                            v.wait_ge(sTV.h, tv0 + c + 1)
                            sRD.inc(v.tensor_copy(XT[:, c * tpc + 128 * tt: c * tpc + 128 * tt + 128],
                                                  pp[2 + c % 2][:, 512:640]))

        def xio_out(pidx):
            for tt in range(ntt):
                tv0, rd0 = sTV.n, sRD.n
                with nc.Block() as blk:
                    @blk.tensor
                    def _(t):
                        for c in range(dc):
                            if c >= 2:
                                t.wait_ge(sRD.h, rd0 + c - 1)
                            sTV.inc(t.transpose(pp[2 + c % 2][:, 512:640],
                                                XT[:, c * tpc + 128 * tt: c * tpc + 128 * tt + 128],
                                                idt[:, :]))

                    @blk.vector
                    def _(v):
                        for c in range(dc):
                            v.wait_ge(sTV.h, tv0 + c + 1)
                            sRD.inc(v.tensor_copy(stg[tt % 2][:, 128 * c:128 * c + 128], pp[2 + c % 2][:, 512:640]))
                with nc.Block() as blk:
                    @blk.sync
                    def _(sy):
                        sOUT.dinc(sy.dma_start(out=xout[pidx, 128 * tt:128 * tt + 128, :], in_=stg[tt % 2][:, :]))

        def dump_xt_at(tag, pidx):
            if DBG.get("dump_xt2") == tag and pidx == DBG.get("pidx"):
                for c in range(dc):
                    dbg_dump(c, XT[:, c * tpc + DBG.get("xt_t0", 0):c * tpc + DBG.get("xt_t0", 0) + 512])

        def attn_sublayer(kind, pidx):
            isa = kind == "a"
            dump_xt_at("pre_ln", pidx)
            ln_stage()
            dump_xt_at("post_ln", pidx)
            if isa:
                qsrc = [W["wqa"][j] for j in range(cfg.dqa // 128)]
                projT([W["wka"][j] for j in range(cfg.dka // 128)], cfg.dka // 128, IB, None, True)
                rms_apply(IB, cfg.dka // 128, cfg.dka, 1e-6, 1.0, knw_sb)
                dma_out_rows(kv["a"]["k_in"][pidx], IB, cfg.dka // 128)
                proj_v([W["wva"][j] for j in range(cfg.dka // 512)], cfg.dka // 512, kv["a"]["v_in"][pidx])
                ow = [W["woa"][j] for j in range(dc)]
            else:
                qsrc = [W["wqb"][j] for j in range(cfg.dvb // 128)]
                projT([W["wkb"][j] for j in range(cfg.dvb // 128)], cfg.dvb // 128, IB, None, False)
                dump_xt_at("post_kproj", pidx)
                dma_out_rows(kv["b"]["k_in"][pidx], IB, cfg.dvb // 128)
                dump_xt_at("post_krows", pidx)
                proj_v([W["wvb"][j] for j in range(cfg.dvb // 512)], cfg.dvb // 512, kv["b"]["v_in"][pidx])
                dump_xt_at("post_vproj", pidx)
                ow = [W["wob"][j] for j in range(dc)]
            kk = kv[kind]
            dk = cfg.dka if isa else cfg.dvb
            with nc.Block() as blk:
                @blk.gpsimd
                def _(g):
                    skv2.wait(g)
                    scc.inc(g.collective_compute("AllGather", OP.bypass,
                                                 replica_groups=[[0, 1], [2, 3], [4, 5], [6, 7]],
                                                 ins=[kk["k_in"][pidx].ap().opt()],
                                                 outs=[kk["k_out"].ap()[pidx].opt()]))
                    scc.inc(g.collective_compute("AllGather", OP.bypass,
                                                 replica_groups=[[0, 1], [2, 3], [4, 5], [6, 7]],
                                                 ins=[kk["v_in"][pidx].ap().opt()],
                                                 outs=[kk["v_out"].ap()[pidx].opt()]))
                    scc.wait(g)
            with nc.Block() as blk:
                @blk.sync
                def _(sy):
                    scc.wait(sy)
                    for rh in (0, 1):
                        for bi, j in enumerate(blocks_of(cfg, rh, pidx)):
                            pos = pidx * cfg.half + j * 512
                            lo = bi * 512
                            skv2.dinc(sy.dma_start(
                                out=bass.AP(kk["kg"], PAD + pos, [[PAD + cfg.s, dk], [1, 512]]),
                                in_=bass.AP(kk["k_out"], pidx * (2 * dk * tpc) + rh * dk * tpc + lo,
                                            [[tpc, dk], [1, 512]])))
                            skv2.dinc(sy.dma_start(
                                out=bass.AP(kk["vg"], (PAD + pos) * dk, [[dk, 512], [1, dk]]),
                                in_=bass.AP(kk["v_out"], pidx * (2 * tpc * dk) + (rh * tpc + lo) * dk,
                                            [[dk, 512], [1, dk]])))
                    skv2.wait(sy)
            dump_xt_at("pre_attn", pidx)
            attention(kind, pidx, ow, (cfg.dqa if isa else cfg.dvb) // 128, qsrc)

        with nc.Block() as blk:
            @blk.vector
            def _(v):
                v.memset(XN[:, 0:PAD], 0.0)
        with nc.Block() as blk:
            @blk.sync
            def _(sy):
                for kind in kinds:
                    dkz = cfg.dka if kind == "a" else cfg.dvb
                    for r0 in range(0, dkz, 128):
                        skv2.dinc(sy.dma_start(
                            out=bass.AP(kv[kind]["kg"], r0 * (PAD + cfg.s), [[PAD + cfg.s, 128], [1, PAD]]),
                            in_=XN[:, 0:PAD]))
                    for r0 in range(0, PAD, 128):
                        skv2.dinc(sy.dma_start(
                            out=bass.AP(kv[kind]["vg"], r0 * dkz, [[dkz, 128], [1, dkz]]),
                            in_=XN[:, 0:dkz]))
                skv2.wait(sy)

        for pidx in range(2):
            xio_in(pidx)
            for sl in cfg.sublayers:
                if sl == "attn_a":
                    attn_sublayer("a", pidx)
                elif sl == "attn_b":
                    attn_sublayer("b", pidx)
                elif sl == "ln":
                    ln_stage()
                    with nc.Block() as blk:
                        @blk.vector
                        def _(v):
                            for c in range(dc):
                                v.tensor_copy(XT[:, c * tpc:(c + 1) * tpc], XN[:, c * tpc:(c + 1) * tpc])
                else:
                    ln_stage()
                    ffn()
            xio_out(pidx)

        with nc.Block() as blk:
            @blk.sync
            def _(sy):
                sOUT.wait(sy)

    return nc


def run(cfg, inputs, trace=False):
    prep = host_prep(cfg, inputs)
    nc = build_program(cfg)
    kinds = active_kinds(cfg)
    in_maps = []
    for c in range(8):
        tb = prep["tab"][c]
        if len(tb) < 4:
            tb = np.concatenate([tb, np.zeros(4 - len(tb), np.int32)])
        m = {"xin": prep["xin"][c], "idf": prep["idf"], "tab": tb[None, :]}
        if "a" in kinds:
            m.update(wqa=prep["wqa"], wka=prep["wka"], wva=prep["wva"], woa=prep["woa"],
                     qnw=prep["qnw"], knw=prep["knw"], bra=prep["bra"][c], sca=prep["sca"][c])
        if "b" in kinds:
            m.update(wqb=prep["wqb"], wkb=prep["wkb"], wvb=prep["wvb"], wob=prep["wob"],
                     brb=prep["brb"][c], scb=prep["scb"][c])
        if "ffn" in cfg.sublayers:
            m.update(pg=prep["pg"][c], sg=prep["sg"][c], pu=prep["pu"][c], su=prep["su"][c],
                     pd=prep["pd"][c], sd=prep["sd"][c])
        in_maps.append(m)
    res = run_bass_kernel_spmd(nc, in_maps, core_ids=list(range(8)), trace=trace)
    s2 = prep["s2"]
    y_perm = np.zeros((cfg.b, cfg.s, cfg.d), np.float32)
    for c in range(8):
        seq = c // 2
        out = np.asarray(res.results[c]["xout"], np.float32)
        for p in range(2):
            bl = blocks_of(cfg, c % 2, p)
            for bi, j in enumerate(bl):
                y_perm[seq, p * cfg.half + j * 512: p * cfg.half + (j + 1) * 512] = \
                    out[p, bi * 512:(bi + 1) * 512]
    y = np.empty_like(y_perm)
    y[:, :, s2] = y_perm
    return y, res


def kernel(**inputs):
    cfg = Cfg()
    y, _ = run(cfg, inputs)
    return y
